# version 32
# baseline (speedup 1.0000x reference)
import os
import numpy as np
from contextlib import ExitStack

import concourse.bass as bass
import concourse.mybir as mybir
from concourse.bass import IndirectOffsetOnAxis
from concourse.bass_utils import run_bass_kernel_spmd

F32 = mybir.dt.float32
BF16 = mybir.dt.bfloat16
AF = mybir.ActivationFunctionType
ALU = mybir.AluOpType
AX = mybir.AxisListType

B_LOC = 4
T = 2048
D = 1024
NT = T // 128
KD = D // 128
H = 16
E = 8
FS = 2048
NFS = FS // 128
NCORE = 8
SCALE = 0.125
SLOT = 256
NSLOT = 23
NCAP = NSLOT * SLOT


class Sem:
    def __init__(self, handle):
        self.h = handle
        self.n = 0


class Plan:
    ENGS = ("tensor", "vector", "scalar", "sync", "gpsimd")

    def __init__(self):
        self.prog = {e: [] for e in self.ENGS}
        self.last_wait = {e: {} for e in self.ENGS}

    def em(self, eng, fn):
        self.prog[eng].append(fn)

    def wait(self, eng, sem, val):
        if sem is None or val <= 0:
            return
        lw = self.last_wait[eng]
        if lw.get(id(sem), -1) >= val:
            return
        lw[id(sem)] = val
        self.em(eng, lambda e, s=sem.h, v=val: e.wait_ge(s, v))

    def inc(self, eng, sem, fn, k=1):
        sem.n += k
        self.em(eng, lambda e, f=fn, s=sem.h, kk=k: f(e).then_inc(s, kk))
        return sem.n


class _StopEmission(Exception):
    pass


def build_program():
    limit = int(os.environ.get("MK_LIMIT", "99999"))
    _ph = [0]

    def ckpt():
        _ph[0] += 1
        if _ph[0] > limit:
            raise _StopEmission()

    nc = bass.Bass("TRN2", target_bir_lowering=False)

    x_h = nc.dram_tensor("x", [B_LOC * T, D], F32, kind="ExternalInput").ap()
    out_h = nc.dram_tensor("out", [B_LOC * T, D], F32, kind="ExternalOutput").ap()
    wq_h = nc.dram_tensor("wq", [D, D], F32, kind="ExternalInput").ap()
    wkv_h = nc.dram_tensor("wkv", [D, D], F32, kind="ExternalInput").ap()
    wout_h = nc.dram_tensor("wout", [D, D], F32, kind="ExternalInput").ap()
    bout_h = nc.dram_tensor("boutr", [1, D], F32, kind="ExternalInput").ap()
    rt_h = nc.dram_tensor("rt", [D, E], F32, kind="ExternalInput").ap()
    id_h = nc.dram_tensor("ident", [128, 128], F32, kind="ExternalInput").ap()
    w1t_h = nc.dram_tensor("w1t", [E, D, D], F32, kind="ExternalInput").ap()
    w2t_h = nc.dram_tensor("w2t", [E, D, D], F32, kind="ExternalInput").ap()
    ws1t_h = nc.dram_tensor("ws1t", [D, FS], F32, kind="ExternalInput").ap()
    ws2t_h = nc.dram_tensor("ws2t", [FS, D], F32, kind="ExternalInput").ap()
    spill_h = nc.dram_tensor("xspill", [T, D], F32, kind="Internal").ap()
    ltri_h = nc.dram_tensor("ltri", [128, 128], F32, kind="ExternalInput").ap()
    cst_h = nc.dram_tensor("cstr", [1, 64], F32, kind="ExternalInput").ap()
    iop_h = nc.dram_tensor("iotap", [128, 2], F32, kind="ExternalInput").ap()
    w1s_h = nc.dram_tensor("w1s", [E * 128, KD * D], BF16, kind="Internal").ap()
    w2s_h = nc.dram_tensor("w2s", [E * 128, KD * D], BF16, kind="Internal").ap()
    xnc_h = nc.dram_tensor("xnc", [NCAP, D], BF16, kind="Internal").ap()
    yc_h = nc.dram_tensor("yc", [NCAP, D], BF16, kind="Internal").ap()

    stack = ExitStack()
    sb = lambda n, s, d: stack.enter_context(nc.sbuf_tensor(n, s, d))
    pst = lambda n, s, d: stack.enter_context(nc.psum_tensor(n, s, d))
    mksem = lambda n: Sem(stack.enter_context(nc.semaphore(n)))

    with stack:
        x_sb = sb("x_sb", [128, NT * D], F32)
        r_a = sb("r_a", [128, KD * T], BF16)
        r_b = sb("r_b", [128, KD * T], BF16)
        r_c = sb("r_c", [128, KD * T], BF16)
        wsl = sb("wsl", [128, KD * T], BF16)
        att0 = sb("att0", [128, T], BF16)
        att1 = sb("att1", [128, T], BF16)
        g_buf = sb("g_buf", [128, 512], BF16)
        bout_r = sb("bout_r", [1, D], BF16)
        ones1 = sb("ones1", [1, 128], BF16)
        rt_sb = sb("rt_sb", [128, KD * E], BF16)
        epsv = sb("epsv", [128, 1], F32)
        stA = sb("stA", [128, 192], F32)
        st2A = sb("st2A", [128, 32], F32)
        lnvA = sb("lnvA", [128, 16], F32)
        rrA = sb("rrA", [128, 16], F32)
        nmrA = sb("nmrA", [128, 16], F32)
        st12 = sb("st12", [128, 12], F32)
        st2 = sb("st2", [128, 2], F32)
        lnv = sb("lnv", [128, 1], F32)
        rr = sb("rr", [128, 1], F32)
        nmr = sb("nmr", [128, 1], F32)
        wts = sb("wts", [128, NT * E], F32)
        lgA = sb("lgA", [128, NT * E], F32)
        bufA = sb("bufA", [128, NT * E], F32)
        m1g = sb("m1g", [128, NT], F32)
        m2g = sb("m2g", [128, NT], F32)
        deng = sb("deng", [128, NT], F32)
        rdeng = sb("rdeng", [128, NT], F32)
        idb = sb("idb", [128, 128], BF16)
        bufB = stA[:, 0:128]
        ltb = sb("ltb", [128, 128], BF16)
        iop = sb("iop_sb", [128, 2], F32)
        onecb = sb("onecb", [128, 1], BF16)
        cstr = sb("cstr_sb", [1, 64], F32)
        w1tok = sb("w1tok", [128, NT], F32)
        w2tok = sb("w2tok", [128, NT], F32)
        d1i = sb("d1i", [128, NT], mybir.dt.int32)
        d2i = sb("d2i", [128, NT], mybir.dt.int32)
        widxi = sb("widxi", [128, 32], mybir.dt.int32)
        iguard = sb("iguard", [128, 1], mybir.dt.int32)
        att1f = att1[:].bitcast(F32)
        m1s = att1f[:, 0:128]
        posf = att1f[:, 128:256]
        dif = att1f[:, 256:384]
        mskb = att1[:, 768:896]
        ctr = att1f[0:1, 448:576]
        cta = att1f[0:1, 576:704]
        tbA = att1f[0:1, 704:832]
        tbB = att1f[0:1, 832:960]
        pA8 = att1f[0:1, 960:968]
        pB8 = att1f[0:1, 968:976]
        reg8 = att1f[0:1, 976:984]
        sexpr = att1f[0:1, 984:1016]
        rowb = stA[0:1, 0:192]
        ones1f = stA[0:1, 64:192]
        widxf = stA[:, 128:160]
        dst1f = stA[:, 160:176]
        dst2f = stA[:, 176:192]

        yt0 = att0[:].bitcast(F32)
        yt1 = att1[:].bitcast(F32)
        xn_tmp0 = att0[:, 0:D]
        xn_tmp1 = att0[:, D:2 * D]

        P2 = [pst(f"p2_{i}", [128, 1024], F32) for i in range(4)]

        LWA = mksem("lwa")
        LWB = mksem("lwb")
        LCA = mksem("lca")
        LCB = mksem("lcb")
        LRT = mksem("lrt")
        LDX = mksem("ldx")
        XSC = mksem("xsc")
        WCS = mksem("wcs")
        TRV = mksem("trv")
        TRW = mksem("trw")
        YWR = mksem("ywr")
        YGQ = mksem("ygq")
        TMR = mksem("tmr")
        SPL = mksem("spl")
        STA = mksem("sta")
        STB = mksem("stb")
        GAA = mksem("gaa")
        GAB = mksem("gab")
        TS = mksem("ts")
        VQ = mksem("vq")
        SQ = mksem("sq")

        P = Plan()
        em, wait, inc = P.em, P.wait, P.inc

        pair_free = [None] * 4

        def pair_wait(i):
            if pair_free[i] is not None:
                deps = pair_free[i]
                if not isinstance(deps, list):
                    deps = [deps]
                for (sm, vv) in deps:
                    wait("tensor", sm, vv)

        def slot(tensor, s):
            return tensor[:, s * KD * D:(s + 1) * KD * D]

        def fmv(t_, k, c0, n):
            return t_[:, k * T + c0:k * T + c0 + n]

        def h1q_ap(half, ft, j0, n):
            return r_b[:, half * 8192 + ft * 1024 + j0:half * 8192 + ft * 1024 + j0 + n]

        def sh_ap(fst, j0, n):
            return r_b[:, KD * 1024 + fst * 512 + j0:KD * 1024 + fst * 512 + j0 + n]

        em("vector", lambda v: v.memset(ones1[:], 1.0))
        inc("vector", VQ, lambda v: v.memset(epsv[:], 1e-5))

        em("gpsimd", lambda g: g.dma_start(out=bout_r[:], in_=bout_h).then_inc(LRT.h, 16))
        LRT.n += 16
        em("gpsimd", lambda g: g.dma_start(
            out=rt_sb[:].rearrange("p (k e) -> p k e", k=KD),
            in_=rt_h.rearrange("(k p) e -> p k e", p=128)).then_inc(LRT.h, 16))
        LRT.n += 16
        em("gpsimd", lambda g: g.dma_start(out=idb[:], in_=id_h).then_inc(LRT.h, 16))
        LRT.n += 16
        em("gpsimd", lambda g: g.dma_start(out=ltb[:], in_=ltri_h).then_inc(LRT.h, 16))
        LRT.n += 16
        em("sync", lambda s: s.dma_start(out=cstr[:], in_=cst_h).then_inc(LRT.h, 16))
        LRT.n += 16
        em("sync", lambda s: s.dma_start(out=iop[:], in_=iop_h).then_inc(LRT.h, 16))
        LRT.n += 16
        rt_rdy = (LRT, LRT.n)
        bout_rdy = (LRT, LRT.n)
        id_rdy = (LRT, LRT.n)
        cst_rdy = (LRT, LRT.n)
        wait("vector", LRT, LRT.n)
        t_onec = inc("vector", VQ, lambda v: v.tensor_copy(onecb[:], iop[:, 1:2]))

        em("sync", lambda s: s.dma_start(
            out=x_sb[:].rearrange("p (n d) -> p n d", n=NT),
            in_=x_h[0:T, :].rearrange("(n p) d -> p n d", p=128)).then_inc(LDX.h, 16))
        LDX.n += 16
        x_ready = [(LDX, LDX.n)] + [None] * (B_LOC - 1)

        def load_w(dst_tensor, s, src_ap, after):
            sem = (LWA if s == 0 else LWB) if dst_tensor is wsl else (LCA if s == 0 else LCB)
            wait("gpsimd", sem, sem.n)
            if after is not None:
                wait("gpsimd", after[0], after[1])
            dst = slot(dst_tensor, s).rearrange("p (k n) -> p k n", k=KD)
            src = src_ap.rearrange("(k p) n -> p k n", p=128)
            em("gpsimd", lambda g, d=dst, sr=src, ss=sem: g.dma_start(out=d, in_=sr).then_inc(ss.h, 16))
            sem.n += 16
            return (sem, sem.n)

        wq_rdy = load_w(wsl, 0, wq_h, None)
        wkv_rdy = load_w(wsl, 1, wkv_h, None)

        for e in range(E):
            for (srct, dstt) in ((w1t_h, w1s_h), (w2t_h, w2s_h)):
                em("gpsimd", lambda g, s_=srct[e], d_=dstt[e * 128:(e + 1) * 128, :]:
                   g.dma_start(out=d_.rearrange("p (k n) -> p k n", k=KD),
                               in_=s_.rearrange("(k p) n -> p k n", p=128)).then_inc(WCS.h, 16))
                WCS.n += 16
        wcs_done = WCS.n

        xn_tmp_free = [None, None]
        yt_store_spq = [None, None]
        x_tile_rdy = [None] * NT

        def emit_ln_phase(tiles, dst_region, aeng="vector", src=None, src_rdy=None):
            if src is None:
                src = lambda t: x_sb[:, t * D:t * D + D]
            if src_rdy is None:
                src_rdy = lambda t: x_tile_rdy[t]
            asem = VQ if aeng == "vector" else SQ
            wait("vector", VQ, VQ.n)
            if yt_store_spq[0] is not None:
                wait(aeng, yt_store_spq[0][0], yt_store_spq[0][1])
            for i, t_i in enumerate(tiles):
                if src_rdy(t_i) is not None:
                    wait("vector", src_rdy(t_i)[0], src_rdy(t_i)[1])
                sl = src(t_i)
                em("vector", lambda v, s=sl, ii=i: v.bn_stats(stA[:, ii * 12:ii * 12 + 6], s[:, 0:512]))
                em("vector", lambda v, s=sl, ii=i: v.bn_stats(stA[:, ii * 12 + 6:ii * 12 + 12], s[:, 512:1024]))
            t = inc("vector", VQ, lambda v: v.engine_nop())
            wait("vector", VQ, t)
            n = len(tiles)
            for i in range(n):
                em("vector", lambda v, ii=i: v.bn_aggr(st2A[:, ii * 2:ii * 2 + 2], stA[:, ii * 12:ii * 12 + 12]))
            t_ag = inc("vector", VQ, lambda v: v.engine_nop())
            wait("scalar", VQ, t_ag)
            t_ln = inc("scalar", SQ, lambda s, nn=n: s.activation(
                lnvA[:, 0:nn], st2A[:, 1:2 * nn:2], AF.Ln, bias=epsv[:], scale=1.0))
            wait("scalar", SQ, t_ln)
            t_rr = inc("scalar", SQ, lambda s, nn=n: s.activation(
                rrA[:, 0:nn], lnvA[:, 0:nn], AF.Exp, bias=0.0, scale=-0.5))
            wait("vector", SQ, t_rr)
            t_nm = inc("vector", VQ, lambda v, nn=n: v.scalar_tensor_tensor(
                nmrA[:, 0:nn], st2A[:, 0:2 * nn:2], -1.0, rrA[:, 0:nn], ALU.mult, ALU.mult))
            wait("vector", VQ, t_nm)
            wait("tensor", id_rdy[0], id_rdy[1])
            wait(aeng, VQ, t_nm)
            for i, t_i in enumerate(tiles):
                par = t_i % 2
                tmp = xn_tmp0 if par == 0 else xn_tmp1
                if xn_tmp_free[par] is not None:
                    wait(aeng, xn_tmp_free[par][0], xn_tmp_free[par][1])
                if aeng == "vector":
                    t_ap = inc("vector", VQ, lambda v, s=src(t_i), d=tmp, ii=i:
                               v.tensor_scalar(d, s, rrA[:, ii:ii + 1], nmrA[:, ii:ii + 1], ALU.mult, ALU.add))
                else:
                    t_ap = inc("scalar", SQ, lambda s2, s=src(t_i), d=tmp, ii=i:
                               s2.activation(d, s, AF.Identity, bias=nmrA[:, ii:ii + 1], scale=rrA[:, ii:ii + 1]))
                pp = par
                pair_wait(pp)
                wait("tensor", asem, t_ap)
                pview = P2[pp][:].bitcast(BF16)
                pcol = lambda kk: kk * 128 if kk < 4 else 1024 + (kk - 4) * 128
                for k in range(KD - 1):
                    em("tensor", lambda t, tm=tmp, kk=k, pv=pview: t.transpose(
                        pv[:, pcol(kk):pcol(kk) + 128], tm[:, kk * 128:kk * 128 + 128], idb[:]))
                t_tr = inc("tensor", TS, lambda t, tm=tmp, pv=pview: t.transpose(
                    pv[:, pcol(KD - 1):pcol(KD - 1) + 128], tm[:, (KD - 1) * 128:KD * 128], idb[:]))
                xn_tmp_free[par] = (TS, t_tr)
                wait("vector", TS, t_tr)
                wait("scalar", TS, t_tr)
                for k in range(3):
                    em("vector", lambda v, kk=k, tt=t_i, dr=dst_region, pv=pview: v.tensor_copy(
                        fmv(dr, kk, tt * 128, 128), pv[:, pcol(kk):pcol(kk) + 128]))
                tv = inc("vector", VQ, lambda v, tt=t_i, dr=dst_region, pv=pview: v.tensor_copy(
                    fmv(dr, 3, tt * 128, 128), pv[:, pcol(3):pcol(3) + 128]))
                for k in range(4, KD - 1):
                    em("scalar", lambda s2, kk=k, tt=t_i, dr=dst_region, pv=pview: s2.copy(
                        fmv(dr, kk, tt * 128, 128), pv[:, pcol(kk):pcol(kk) + 128]))
                ts_ = inc("scalar", SQ, lambda s2, tt=t_i, dr=dst_region, pv=pview: s2.copy(
                    fmv(dr, KD - 1, tt * 128, 128), pv[:, pcol(KD - 1):pcol(KD - 1) + 128]))
                pair_free[pp] = [(VQ, tv), (SQ, ts_)]
            return (VQ.n, SQ.n)
        def emit_proj_half(wslot_i, dst_region, scale, half, ln_v, ln_s, w_rdy,
                           guard_ts=None, guard_sems=(), evac_eng="split"):
            wait("tensor", VQ, ln_v)
            wait("tensor", SQ, ln_s)
            wait("tensor", w_rdy[0], w_rdy[1])
            tok0 = half * 1024
            first_evac = [True]
            for dt in range(KD):
                p_ = dt % 2
                pair_wait(p_)
                for k in range(KD):
                    st_ap = slot(wsl, wslot_i)[:, k * D + dt * 128:k * D + dt * 128 + 128]
                    for ci in range(2):
                        fn = lambda t, a=st_ap, kk=k, c2=ci, pq=p_, t0=tok0: t.matmul(
                            P2[pq][:, c2 * 512:c2 * 512 + 512], a,
                            fmv(r_a, kk, t0 + c2 * 512, 512),
                            start=(kk == 0), stop=(kk == KD - 1))
                        if k == KD - 1 and ci == 1:
                            t_mm = inc("tensor", TS, fn)
                        else:
                            em("tensor", fn)
                if evac_eng == "scalar":
                    eng, sem = "scalar", SQ
                else:
                    eng = "vector" if dt % 2 == 0 else "scalar"
                    sem = VQ if dt % 2 == 0 else SQ
                wait(eng, TS, t_mm)
                if first_evac[0]:
                    first_evac[0] = False
                    if guard_ts is not None:
                        wait("vector", TS, guard_ts)
                        wait("scalar", TS, guard_ts)
                    for gs in guard_sems:
                        wait("vector", gs[0], gs[1])
                        wait("scalar", gs[0], gs[1])
                d_ = fmv(dst_region, dt, tok0, 1024)
                if scale == 1.0:
                    if eng == "vector":
                        tval = inc("vector", VQ, lambda v, d=d_, pq=p_: v.tensor_copy(d, P2[pq][:]))
                    else:
                        tval = inc("scalar", SQ, lambda s, d=d_, pq=p_: s.copy(d, P2[pq][:]))
                else:
                    if eng == "vector":
                        tval = inc("vector", VQ, lambda v, d=d_, pq=p_, sc=scale:
                                   v.tensor_scalar(d, P2[pq][:], sc, None, ALU.mult))
                    else:
                        tval = inc("scalar", SQ, lambda s, d=d_, pq=p_, sc=scale:
                                   s.mul(d, P2[pq][:], sc))
                pair_free[p_] = (sem, tval)
        carry = {}
        try:
          for b in range(B_LOC):
            if b == 0:
                wait("vector", x_ready[b][0], x_ready[b][1])
                ln1_rdy = emit_ln_phase(list(range(NT)), r_a)
                ckpt()
                emit_proj_half(0, r_b, SCALE, 0, ln1_rdy[0], ln1_rdy[1], wq_rdy)
                emit_proj_half(1, r_c, 1.0, 0, ln1_rdy[0], ln1_rdy[1], wkv_rdy)
                emit_proj_half(0, r_b, SCALE, 1, ln1_rdy[0], ln1_rdy[1], wq_rdy)
                emit_proj_half(1, r_c, 1.0, 1, ln1_rdy[0], ln1_rdy[1], wkv_rdy)
                qkv_ts = TS.n
                kv_evac_v, kv_evac_s = VQ.n, SQ.n
            else:
                qkv_ts = carry["qkv_ts"]
                kv_evac_v = carry["kv_evac_v"]
                kv_evac_s = carry["kv_evac_s"]

            wout_rdy = load_w(wsl, 0, wout_h, after=(TS, qkv_ts))

            ckpt()
            v_tile_rdy = [None] * NT
            for tq, tsem in (("sync", TRV),):
                wait(tq, tsem, tsem.n)
                wait(tq, VQ, kv_evac_v)
                wait(tq, SQ, kv_evac_s)
                wait(tq, TS, qkv_ts)
            for t_i in range(NT):
                tsem = TRV
                tq = "sync"
                wait(tq, tsem, tsem.n)
                for k in range(KD):
                    em(tq, lambda s, kk=k, tt=t_i, ss=tsem: s.dma_start_transpose(
                        out=r_a[:, tt * D + kk * 128:tt * D + kk * 128 + 128],
                        in_=fmv(r_c, kk, tt * 128, 128)).then_inc(ss.h, 16))
                    tsem.n += 16
                v_tile_rdy[t_i] = (tsem, tsem.n)

            ckpt()
            for par in range(2):
                if yt_store_spq[par] is not None:
                    wait("vector", yt_store_spq[par][0], yt_store_spq[par][1])
                    wait("scalar", yt_store_spq[par][0], yt_store_spq[par][1])
            qk_bank = [None] * 4
            for hp in range(H // 2):
                h0 = 2 * hp
                its = [(si, hc) for si in range(NT) for hc in range(2)]
                prev = None
                for (s_i, hc) in its:
                    if s_i == 0 and hc == 0:
                        if hp == 0:
                            pair_wait(0)
                            pair_wait(1)
                        pair_wait(2)
                        pair_wait(3)
                    kst0 = fmv(r_c, hp, s_i * 128, 128)[0:64, :]
                    kst1 = fmv(r_c, hp, s_i * 128, 128)[64:128, :]
                    qk_t = [0, 0, 0, 0]
                    for ci in range(2):
                        c = 2 * hc + ci
                        q0 = fmv(r_b, hp, c * 512, 512)[0:64, :]
                        q1 = fmv(r_b, hp, c * 512, 512)[64:128, :]
                        if qk_bank[2 * ci] is not None:
                            wait("tensor", qk_bank[2 * ci][0], qk_bank[2 * ci][1])
                        fn0 = lambda t, a=kst0, qq=q0, c2=ci: t.matmul(
                            P2[0][:, c2 * 512:c2 * 512 + 512], a, qq, start=True, stop=True,
                            tile_position=(0, 0))
                        qk_t[2 * ci] = inc("tensor", TS, fn0)
                        if qk_bank[2 * ci + 1] is not None:
                            wait("tensor", qk_bank[2 * ci + 1][0], qk_bank[2 * ci + 1][1])
                        mmfn = lambda t, a=kst1, qq=q1, c2=ci: t.matmul(
                            P2[1][:, c2 * 512:c2 * 512 + 512], a, qq, start=True, stop=True,
                            tile_position=(64, 0))
                        qk_t[2 * ci + 1] = inc("tensor", TS, mmfn)
                    t_sc = qk_t[3]
                    if prev is not None:
                        pv, ps_, p_si, p_hc = prev
                        wait("tensor", VQ, pv)
                        wait("tensor", SQ, ps_)
                        wait("tensor", v_tile_rdy[p_si][0], v_tile_rdy[p_si][1])
                        vst0 = r_a[:, p_si * D + h0 * 64:p_si * D + h0 * 64 + 64]
                        vst1 = r_a[:, p_si * D + h0 * 64 + 64:p_si * D + h0 * 64 + 128]
                        pav = 2 + p_hc
                        for ci in range(2):
                            c = 2 * p_hc + ci
                            em("tensor", lambda t, a=vst0, cc=c, c2=ci, pq=pav, fi=(p_si == 0), la=(p_si == NT - 1):
                               t.matmul(P2[pq][0:64, c2 * 512:c2 * 512 + 512], a,
                                        att0[:, cc * 512:cc * 512 + 512], start=fi, stop=la,
                                        skip_group_check=True))
                            em("tensor", lambda t, a=vst1, cc=c, c2=ci, pq=pav, fi=(p_si == 0), la=(p_si == NT - 1):
                               t.matmul(P2[pq][64:128, c2 * 512:c2 * 512 + 512], a,
                                        att1[:, cc * 512:cc * 512 + 512], start=fi, stop=la,
                                        tile_position=(0, 64), skip_group_check=True))
                    wait("vector", TS, qk_t[0])
                    v1 = inc("vector", VQ, lambda v, o0=hc * 1024:
                             v.tensor_scalar_max(att0[:, o0:o0 + 512], P2[0][:, 0:512], 0.0))
                    wait("scalar", TS, qk_t[1])
                    s1 = inc("scalar", SQ, lambda s2, o0=hc * 1024:
                             s2.activation(att1[:, o0:o0 + 512], P2[1][:, 0:512], AF.Relu, bias=0.0, scale=1.0))
                    wait("vector", TS, qk_t[2])
                    tv = inc("vector", VQ, lambda v, o0=hc * 1024:
                             v.tensor_scalar_max(att0[:, o0 + 512:o0 + 1024], P2[0][:, 512:1024], 0.0))
                    wait("scalar", TS, qk_t[3])
                    ts_ = inc("scalar", SQ, lambda s2, o0=hc * 1024:
                              s2.activation(att1[:, o0 + 512:o0 + 1024], P2[1][:, 512:1024], AF.Relu, bias=0.0, scale=1.0))
                    qk_bank[0] = (VQ, v1)
                    qk_bank[1] = (SQ, s1)
                    qk_bank[2] = (VQ, tv)
                    qk_bank[3] = (SQ, ts_)
                    pair_free[0] = (VQ, tv)
                    pair_free[1] = (SQ, ts_)
                    prev = (tv, ts_, s_i, hc)
                pv, ps_, p_si, p_hc = prev
                wait("tensor", VQ, pv)
                wait("tensor", SQ, ps_)
                wait("tensor", v_tile_rdy[p_si][0], v_tile_rdy[p_si][1])
                vst0 = r_a[:, p_si * D + h0 * 64:p_si * D + h0 * 64 + 64]
                vst1 = r_a[:, p_si * D + h0 * 64 + 64:p_si * D + h0 * 64 + 128]
                pav = 2 + p_hc
                for ci in range(2):
                    c = 2 * p_hc + ci
                    em("tensor", lambda t, a=vst0, cc=c, c2=ci, pq=pav:
                       t.matmul(P2[pq][0:64, c2 * 512:c2 * 512 + 512], a,
                                att0[:, cc * 512:cc * 512 + 512], start=False, stop=True,
                                skip_group_check=True))
                    mmfn = lambda t, a=vst1, cc=c, c2=ci, pq=pav: t.matmul(
                        P2[pq][64:128, c2 * 512:c2 * 512 + 512], a,
                        att1[:, cc * 512:cc * 512 + 512], start=False, stop=True,
                        tile_position=(0, 64), skip_group_check=True)
                    if ci == 1:
                        t_av = inc("tensor", TS, mmfn)
                    else:
                        em("tensor", mmfn)
                wait("vector", TS, t_av)
                wait("scalar", TS, t_av)
                tv = inc("vector", VQ, lambda v, d=fmv(r_b, hp, 0, 1024): v.tensor_copy(d, P2[2][:]))
                ts2 = inc("scalar", SQ, lambda s2, d=fmv(r_b, hp, 1024, 1024): s2.copy(d, P2[3][:]))
                pair_free[2] = (VQ, tv)
                pair_free[3] = (SQ, ts2)
            wait("tensor", SQ, SQ.n)
            heads_ts = TS.n
            ao_done_v = VQ.n

            ckpt()
            wait("tensor", VQ, ao_done_v)
            wait("tensor", wout_rdy[0], wout_rdy[1])
            wait("tensor", bout_rdy[0], bout_rdy[1])
            wait("tensor", VQ, 1)
            wait("vector", TS, heads_ts)
            ln2_half_rdy = [None, None]

            def emit_ln2_half(half):
                wait("vector", TS, heads_ts)
                wait("scalar", TS, heads_ts)
                wait("sync", SPL, SPL.n)
                wait("sync", GAA, GAA.n)
                wait("sync", GAB, GAB.n)
                tiles = list(range(half * 8, half * 8 + 8))
                ln2_half_rdy[half] = emit_ln_phase(tiles, r_a)
                wait("sync", VQ, ln2_half_rdy[half][0])
                for t_i in tiles:
                    em("sync", lambda s, tt=t_i: s.dma_start(
                        out=spill_h[tt * 128:tt * 128 + 128, :],
                        in_=x_sb[:, tt * D:tt * D + D]).then_inc(SPL.h, 16))
                    SPL.n += 16

            for t_i in range(NT):
                pp = t_i % 2
                pair_wait(pp)
                for k in range(KD):
                    st_ap = fmv(r_b, k, t_i * 128, 128)
                    for ci in range(2):
                        em("tensor", lambda t, a=st_ap, kk=k, c2=ci, p_=pp: t.matmul(
                            P2[p_][:, c2 * 512:c2 * 512 + 512], a,
                            slot(wsl, 0)[:, kk * D + c2 * 512:kk * D + c2 * 512 + 512],
                            start=(kk == 0), stop=False))
                t_mma = inc("tensor", TS, lambda t, p_=pp: t.matmul(
                    P2[p_][:, 0:512], ones1[:], bout_r[:, 0:512], start=False, stop=True))
                t_mm = inc("tensor", TS, lambda t, p_=pp: t.matmul(
                    P2[p_][:, 512:1024], ones1[:], bout_r[:, 512:1024], start=False, stop=True))
                wait("vector", TS, t_mma)
                inc("vector", VQ, lambda v, p_=pp: v.bn_stats(st12[:, 0:6], P2[p_][:, 0:512]))
                wait("vector", TS, t_mm)
                t = inc("vector", VQ, lambda v, p_=pp: v.bn_stats(st12[:, 6:12], P2[p_][:, 512:1024]))
                wait("vector", VQ, t)
                t_ag = inc("vector", VQ, lambda v: v.bn_aggr(st2[:], st12[:]))
                wait("scalar", VQ, t_ag)
                t_ln = inc("scalar", SQ, lambda s: s.activation(lnv[:], st2[:, 1:2], AF.Ln, bias=epsv[:], scale=1.0))
                wait("scalar", SQ, t_ln)
                t_rr = inc("scalar", SQ, lambda s: s.activation(rr[:], lnv[:], AF.Exp, bias=0.0, scale=-0.5))
                wait("vector", SQ, t_rr)
                t_nm = inc("vector", VQ, lambda v: v.scalar_tensor_tensor(
                    nmr[:], st2[:, 0:1], -1.0, rr[:], ALU.mult, ALU.mult))
                xt_ = x_sb[:, t_i * D:t_i * D + D]
                if x_tile_rdy[t_i] is not None:
                    wait("scalar", x_tile_rdy[t_i][0], x_tile_rdy[t_i][1])
                wait("scalar", VQ, t_nm)
                t_b = inc("scalar", SQ, lambda s, d=xt_: s.activation(
                    d, d, AF.Identity, bias=nmr[:], scale=1.0))
                wait("vector", SQ, t_b)
                t_y = inc("vector", VQ, lambda v, d=xt_, p_=pp: v.scalar_tensor_tensor(
                    d, P2[p_][:], rr[:], d, ALU.mult, ALU.add))
                pair_free[pp] = (VQ, t_y)
                if t_i == 7:
                    emit_ln2_half(0)
            wout_ts = TS.n

            ckpt()
            emit_ln2_half(1)
            spill_done = SPL.n
            ln2_tr_ts = TS.n

            ckpt()
            moe_state = {"g_free_v": 0}

            wait("tensor", VQ, ln2_half_rdy[1][0])
            wait("tensor", SQ, ln2_half_rdy[1][1])
            wait("tensor", rt_rdy[0], rt_rdy[1])
            for t_i in range(NT):
                pair_wait(2)
                for k in range(KD - 1):
                    st_ap = fmv(r_a, k, t_i * 128, 128)
                    em("tensor", lambda t, a=st_ap, kk=k: t.matmul(
                        P2[2][:, 0:8], a, rt_sb[:, kk * E:kk * E + E],
                        start=(kk == 0), stop=False))
                st_ap = fmv(r_a, KD - 1, t_i * 128, 128)
                t_mm = inc("tensor", TS, lambda t, a=st_ap: t.matmul(
                    P2[2][:, 0:8], a, rt_sb[:, (KD - 1) * E:KD * E], start=False, stop=True))
                wait("vector", TS, t_mm)
                t_lg = inc("vector", VQ, lambda v, tt=t_i: v.tensor_copy(
                    lgA[:, tt * E:tt * E + E], P2[2][:, 0:8]))
                pair_free[2] = (VQ, t_lg)
            grp = lambda a: a.rearrange("p (g e) -> p g e", g=NT)
            bc = lambda a: a.unsqueeze(2).broadcast_to([128, NT, E])
            t = VQ.n
            wait("vector", VQ, t)
            t = inc("vector", VQ, lambda v: v.reduce_max(m1g[:], grp(lgA[:]), axis=AX.X))
            wait("vector", VQ, t)
            t = inc("vector", VQ, lambda v: v.tensor_tensor(
                grp(bufA[:]), grp(lgA[:]), bc(m1g[:]), ALU.subtract))
            wait("scalar", VQ, t)
            t_ex = inc("scalar", SQ, lambda s: s.activation(bufB, bufA[:], AF.Exp, bias=0.0, scale=1.0))
            wait("vector", SQ, t_ex)
            t = inc("vector", VQ, lambda v: v.tensor_tensor(
                grp(m1s[:]), grp(lgA[:]), bc(m1g[:]), ALU.is_equal))
            wait("vector", VQ, t)
            t = inc("vector", VQ, lambda v: v.scalar_tensor_tensor(
                wts[:], m1s[:], -1e30, lgA[:], ALU.mult, ALU.add))
            wait("vector", VQ, t)
            t = inc("vector", VQ, lambda v: v.reduce_max(m2g[:], grp(wts[:]), axis=AX.X))
            wait("vector", VQ, t)
            t = inc("vector", VQ, lambda v: v.tensor_tensor(
                grp(bufA[:]), grp(lgA[:]), bc(m2g[:]), ALU.is_ge))
            wait("vector", VQ, t)
            t_mskb = inc("vector", VQ, lambda v: v.tensor_copy(mskb[:], bufA[:]))
            wait("vector", VQ, t_mskb)
            t = inc("vector", VQ, lambda v: v.tensor_mul(lgA[:], bufB, bufA[:]))
            wait("vector", VQ, t)
            t = inc("vector", VQ, lambda v: v.reduce_sum(deng[:], grp(lgA[:]), axis=AX.X))
            wait("vector", VQ, t)
            t = inc("vector", VQ, lambda v: v.reciprocal(rdeng[:], deng[:]))
            wait("vector", VQ, t)
            t_wts = inc("vector", VQ, lambda v: v.tensor_tensor(
                grp(wts[:]), grp(lgA[:]), bc(rdeng[:]), ALU.mult))

            ckpt()
            ws1a = load_w(r_c, 0, ws1t_h[:, 0:1024], after=(TS, heads_ts))
            ws1b = load_w(r_c, 1, ws1t_h[:, 1024:2048], after=(TS, heads_ts))
            ws2a = load_w(wsl, 0, ws2t_h[0:1024, :], after=(TS, wout_ts))
            ws2b = load_w(wsl, 1, ws2t_h[1024:2048, :], after=(TS, qkv_ts))
            wait("tensor", ws1a[0], ws1a[1])
            wait("tensor", ws1b[0], ws1b[1])
            x_pref_base = LDX.n

            def emit_final(tc):
                    wait("vector", VQ, VQ.n)
                    tiles = [tc * 4 + i for i in range(4)]
                    for i, t_i in enumerate(tiles):
                        sl = x_sb[:, t_i * D:t_i * D + D]
                        em("vector", lambda v, s=sl, ii=i: v.bn_stats(stA[:, ii * 12:ii * 12 + 6], s[:, 0:512]))
                        em("vector", lambda v, s=sl, ii=i: v.bn_stats(stA[:, ii * 12 + 6:ii * 12 + 12], s[:, 512:1024]))
                    t = inc("vector", VQ, lambda v: v.engine_nop())
                    wait("vector", VQ, t)
                    for i in range(4):
                        em("vector", lambda v, ii=i: v.bn_aggr(st2A[:, ii * 2:ii * 2 + 2], stA[:, ii * 12:ii * 12 + 12]))
                    t_ag = inc("vector", VQ, lambda v: v.engine_nop())
                    wait("scalar", VQ, t_ag)
                    t_ln = inc("scalar", SQ, lambda s: s.activation(
                        lnvA[:, 0:4], st2A[:, 1:8:2], AF.Ln, bias=epsv[:], scale=1.0))
                    wait("scalar", SQ, t_ln)
                    t_rr = inc("scalar", SQ, lambda s: s.activation(
                        rrA[:, 0:4], lnvA[:, 0:4], AF.Exp, bias=0.0, scale=-0.5))
                    wait("vector", SQ, t_rr)
                    t_nm = inc("vector", VQ, lambda v: v.scalar_tensor_tensor(
                        nmrA[:, 0:4], st2A[:, 0:8:2], -1.0, rrA[:, 0:4], ALU.mult, ALU.mult))
                    wait("vector", VQ, t_nm)
                    wait("scalar", VQ, t_nm)
                    for i, t_i in enumerate(tiles):
                        par = t_i % 2
                        yt = yt0 if par == 0 else yt1
                        gsem = GAA if par == 0 else GAB
                        ssem = STA if par == 0 else STB
                        if yt_store_spq[par] is not None:
                            wait("scalar", yt_store_spq[par][0], yt_store_spq[par][1])
                        if par == 0 and xn_tmp_free[0] is not None:
                            wait("scalar", xn_tmp_free[0][0], xn_tmp_free[0][1])
                        if par == 0 and xn_tmp_free[1] is not None:
                            wait("scalar", xn_tmp_free[1][0], xn_tmp_free[1][1])
                        t_ap = inc("scalar", SQ, lambda s2, s=x_sb[:, t_i * D:t_i * D + D], d=yt, ii=i:
                                   s2.activation(d, s, AF.Identity, bias=nmrA[:, ii:ii + 1], scale=rrA[:, ii:ii + 1]))
                        wait("gpsimd", gsem, gsem.n)
                        wait("gpsimd", SPL, spill_done)
                        wait("gpsimd", SQ, t_ap)
                        em("gpsimd", lambda g, tt_=t_i, d=yt, gs=gsem: g.dma_start(
                            out=d[:], in_=spill_h[tt_ * 128:tt_ * 128 + 128, :],
                            accum_op=ALU.add).then_inc(gs.h, 16))
                        gsem.n += 16
                        wait("sync", ssem, ssem.n)
                        wait("sync", gsem, gsem.n)
                        em("sync", lambda s, tt_=t_i, bb_=b, d=yt, ss=ssem: s.dma_start(
                            out=out_h[bb_ * T + tt_ * 128:bb_ * T + tt_ * 128 + 128, :],
                            in_=d[:]).then_inc(ss.h, 16))
                        ssem.n += 16
                        yt_store_spq[par] = (ssem, ssem.n)
                        if b + 1 < B_LOC:
                            wait("sync", LDX, LDX.n)
                            wait("sync", SQ, t_ap)
                            em("sync", lambda s, tt_=t_i, bb_=b: s.dma_start(
                                out=x_sb[:, tt_ * D:tt_ * D + D],
                                in_=x_h[(bb_ + 1) * T + tt_ * 128:(bb_ + 1) * T + tt_ * 128 + 128, :]
                                ).then_inc(LDX.h, 16))
                            LDX.n += 16
                            x_tile_rdy[t_i] = (LDX, LDX.n)



            tmv = lambda i: r_b[:, i * 1024:(i + 1) * 1024]
            xnsv = lambda k: r_b[:, 2048 + k * SLOT:2048 + (k + 1) * SLOT]
            ystv = lambda i: r_b[:, 4096 + i * 1024:4096 + (i + 1) * 1024]
            gbv = lambda j: r_b[:, j * 2048 + 1024:(j + 1) * 2048]
            h1sv = lambda ft: att1[:, ft * SLOT:(ft + 1) * SLOT]
            pc8 = lambda kk: kk * 128 if kk < 4 else 1024 + (kk - 4) * 128
            cstate = {}
            xsc_free = [0, 0]

            def emit_cnt_mms():
                pair_wait(3)
                wait("tensor", VQ, t_mskb)
                wait("tensor", cst_rdy[0], cst_rdy[1])
                wait("tensor", VQ, t_onec)
                em("tensor", lambda t: t.matmul(P2[3][:, 0:128], ltb[:], mskb[:], start=True, stop=True))
                cstate["cnt_ts"] = inc("tensor", TS, lambda t: t.matmul(
                    P2[3][0:1, 128:256], onecb[:], mskb[:], start=True, stop=True))

            def emit_counts_chain():
                wait("vector", TS, cstate["cnt_ts"])
                t = inc("vector", VQ, lambda v: v.tensor_copy(posf[:], P2[3][:, 0:128]))
                wait("vector", VQ, t)
                t = inc("vector", VQ, lambda v: v.tensor_copy(ctr[:], P2[3][0:1, 128:256]))
                wait("vector", VQ, t)
                t = inc("vector", VQ, lambda v: v.tensor_copy(cta[:], ctr[:]))
                wait("vector", VQ, t)
                for s_ in (64, 32, 16, 8):
                    t = inc("vector", VQ, lambda v, ss=s_: v.tensor_tensor(
                        cta[:, 0:ss], cta[:, 0:ss], cta[:, ss:2 * ss], ALU.add))
                    wait("vector", VQ, t)
                r3 = rowb[:, 0:128].rearrange("p (a c) -> p a c", a=8)
                t = inc("vector", VQ, lambda v: v.tensor_tensor(
                    r3, cta[:, 0:8].unsqueeze(2).broadcast_to([1, 8, 16]),
                    cstr[:, 0:16].unsqueeze(1).broadcast_to([1, 8, 16]), ALU.is_gt))
                wait("vector", VQ, t)
                t = inc("vector", VQ, lambda v: v.reduce_sum(reg8[:], r3, axis=AX.X))
                wait("vector", VQ, t)
                t = inc("vector", VQ, lambda v: v.tensor_scalar(
                    reg8[:], reg8[:], float(SLOT), None, ALU.mult))
                wait("vector", VQ, t)
                em("vector", lambda v: v.memset(pA8[:], 0.0))
                t = inc("vector", VQ, lambda v: v.tensor_copy(pA8[:, 1:8], reg8[:, 0:7]))
                wait("vector", VQ, t)
                pp_ = [pA8, pB8]
                for i, s_ in enumerate((1, 2, 4)):
                    src, dst = pp_[i % 2], pp_[(i + 1) % 2]
                    em("vector", lambda v, s2=src, d2=dst, ss=s_: v.tensor_copy(d2[:, 0:ss], s2[:, 0:ss]))
                    t = inc("vector", VQ, lambda v, s2=src, d2=dst, ss=s_: v.tensor_tensor(
                        d2[:, ss:8], s2[:, ss:8], s2[:, 0:8 - ss], ALU.add))
                    wait("vector", VQ, t)
                em("vector", lambda v: v.memset(tbA[:], 0.0))
                t = inc("vector", VQ, lambda v: v.tensor_copy(tbA[:, 8:128], ctr[:, 0:120]))
                wait("vector", VQ, t)
                tt_ = [tbA, tbB]
                for i, s_ in enumerate((8, 16, 32, 64)):
                    src, dst = tt_[i % 2], tt_[(i + 1) % 2]
                    em("vector", lambda v, s2=src, d2=dst, ss=s_: v.tensor_copy(d2[:, 0:ss], s2[:, 0:ss]))
                    t = inc("vector", VQ, lambda v, s2=src, d2=dst, ss=s_: v.tensor_tensor(
                        d2[:, ss:128], s2[:, ss:128], s2[:, 0:128 - ss], ALU.add))
                    wait("vector", VQ, t)
                t = inc("vector", VQ, lambda v: v.tensor_tensor(
                    tbA[:].rearrange("p (t e) -> p t e", t=NT),
                    tbA[:].rearrange("p (t e) -> p t e", t=NT),
                    pB8[:].unsqueeze(1).broadcast_to([1, NT, 8]), ALU.add))
                wait("vector", VQ, t)
                s3 = rowb[:, 0:NSLOT * 8].rearrange("p (a c) -> p a c", a=NSLOT)
                t = inc("vector", VQ, lambda v: v.tensor_tensor(
                    s3, cstr[:, 32:32 + NSLOT].unsqueeze(2).broadcast_to([1, NSLOT, 8]),
                    pB8[:].unsqueeze(1).broadcast_to([1, NSLOT, 8]), ALU.is_gt))
                wait("vector", VQ, t)
                t = inc("vector", VQ, lambda v: v.reduce_sum(sexpr[:, 0:NSLOT], s3, axis=AX.X))
                wait("vector", VQ, t)
                t = inc("vector", VQ, lambda v: v.tensor_scalar(
                    sexpr[:, 0:NSLOT], sexpr[:, 0:NSLOT], -1.0, None, ALU.add))
                cstate["tb_v"] = t

            def emit_broadcasts():
                t_of = inc("vector", VQ, lambda v: v.memset(ones1f, 1.0))
                wait("tensor", VQ, cstate["tb_v"])
                wait("tensor", VQ, t_of)
                em("tensor", lambda t: t.matmul(P2[3][:, 256:384], ones1f, tbA, start=True, stop=True))
                cstate["bb_ts"] = inc("tensor", TS, lambda t: t.matmul(
                    P2[3][:, 384:384 + NSLOT], ones1f, sexpr[:, 0:NSLOT], start=True, stop=True))

            def emit_dest_chain():
                g3 = lambda a: a.rearrange("p (t e) -> p t e", t=NT)
                wait("vector", TS, cstate["bb_ts"])
                t = inc("vector", VQ, lambda v: v.tensor_tensor(
                    dif[:], posf[:], P2[3][:, 256:384], ALU.add))
                wait("vector", VQ, t)
                t = inc("vector", VQ, lambda v: v.tensor_scalar(
                    widxf[:, 0:NSLOT], P2[3][:, 384:384 + NSLOT], 128.0, iop[:, 0:1], ALU.mult, ALU.add))
                wait("vector", VQ, t)
                pair_free[3] = (VQ, t)
                t = inc("vector", VQ, lambda v: v.tensor_copy(widxi[:, 0:NSLOT], widxf[:, 0:NSLOT]))
                wait("vector", VQ, t)
                t = inc("vector", VQ, lambda v: v.tensor_mul(bufB, m1s[:], dif[:]))
                wait("vector", VQ, t)
                t = inc("vector", VQ, lambda v: v.reduce_sum(dst1f[:], g3(bufB), axis=AX.X))
                wait("vector", VQ, t)
                t = inc("vector", VQ, lambda v: v.tensor_mul(bufB, bufA[:], dif[:]))
                wait("vector", VQ, t)
                t = inc("vector", VQ, lambda v: v.reduce_sum(dst2f[:], g3(bufB), axis=AX.X))
                wait("vector", VQ, t)
                t = inc("vector", VQ, lambda v: v.tensor_tensor(
                    dst2f[:], dst2f[:], dst1f[:], ALU.subtract))
                wait("vector", VQ, t)
                wait("vector", VQ, t_wts)
                t = inc("vector", VQ, lambda v: v.tensor_mul(bufB, m1s[:], wts[:]))
                wait("vector", VQ, t)
                t = inc("vector", VQ, lambda v: v.reduce_sum(w1tok[:], g3(bufB), axis=AX.X))
                wait("vector", VQ, t)
                t = inc("vector", VQ, lambda v: v.tensor_scalar(
                    w2tok[:], w1tok[:], -1.0, 1.0, ALU.mult, ALU.add))
                wait("vector", VQ, t)
                t = inc("vector", VQ, lambda v: v.tensor_copy(d1i[:], dst1f[:]))
                wait("vector", VQ, t)
                t = inc("vector", VQ, lambda v: v.tensor_copy(d2i[:], dst2f[:]))
                wait("vector", VQ, t)
                t = inc("vector", VQ, lambda v: v.tensor_copy(iguard[:], d2i[:, 0:1]))
                wait("vector", VQ, t)
                t = inc("vector", VQ, lambda v: v.tensor_copy(iguard[:], widxi[:, 0:1]))
                cstate["guard_v"] = t

            def emit_xtrans(t_i):
                par = t_i % 2
                xst = att0[:, par * 1024:(par + 1) * 1024]
                pp = par
                pair_wait(pp)
                wait("tensor", id_rdy[0], id_rdy[1])
                pview = P2[pp][:].bitcast(BF16)
                for k in range(KD):
                    fn = lambda t, s2=fmv(r_a, k, t_i * 128, 128), pv=pview, cc=pc8(k): \
                        t.transpose(pv[:, cc:cc + 128], s2, idb[:])
                    if k == KD - 1:
                        t_tr = inc("tensor", TS, fn)
                    else:
                        em("tensor", fn)
                wait("vector", TS, t_tr)
                wait("scalar", TS, t_tr)
                wait("vector", XSC, xsc_free[par])
                wait("scalar", XSC, xsc_free[par])
                tv = ts2 = 0
                for k in range(KD):
                    dst = xst[:, k * 128:(k + 1) * 128]
                    if k < 4:
                        fn_v = lambda v, d=dst, pv=pview, cc=pc8(k): v.tensor_copy(d, pv[:, cc:cc + 128])
                        if k == 3:
                            tv = inc("vector", VQ, fn_v)
                        else:
                            em("vector", fn_v)
                    else:
                        fn_s = lambda sc, d=dst, pv=pview, cc=pc8(k): sc.copy(d, pv[:, cc:cc + 128])
                        if k == KD - 1:
                            ts2 = inc("scalar", SQ, fn_s)
                        else:
                            em("scalar", fn_s)
                pair_free[pp] = [(VQ, tv), (SQ, ts2)]
                wait("gpsimd", VQ, tv)
                wait("gpsimd", SQ, ts2)
                wait("gpsimd", VQ, cstate["guard_v"])
                for di_ in (d1i, d2i):
                    em("gpsimd", lambda g, c=di_[:, t_i:t_i + 1], s2=xst:
                       g.indirect_dma_start(out=xnc_h, out_offset=IndirectOffsetOnAxis(ap=c, axis=0),
                                            in_=s2, in_offset=None).then_inc(XSC.h, 16))
                    XSC.n += 16
                xsc_free[par] = XSC.n

            for tc in range(4):
                tok0 = tc * 512
                wait("vector", TS, wout_ts)
                wait("scalar", TS, wout_ts)
                for fst in range(NFS):
                    p_ = fst % 2
                    pair_wait(p_)
                    wsl1 = slot(r_c, fst // 8)
                    fo = (fst % 8) * 128
                    for k in range(KD - 1):
                        st_ap = wsl1[:, k * D + fo:k * D + fo + 128]
                        em("tensor", lambda t, a=st_ap, kk=k, pq=p_, t0=tok0: t.matmul(
                            P2[pq][:, 0:512], a, fmv(r_a, kk, t0, 512), start=(kk == 0), stop=False))
                    st_ap = wsl1[:, (KD - 1) * D + fo:(KD - 1) * D + fo + 128]
                    t_mm = inc("tensor", TS, lambda t, a=st_ap, pq=p_, t0=tok0: t.matmul(
                        P2[pq][:, 0:512], a, fmv(r_a, KD - 1, t0, 512), start=False, stop=True))
                    if tc == 3 and fst == 7:
                        sh_h0_ts = TS.n
                    if tc == 3 and fst == 15:
                        sh_h1_ts = TS.n
                    wait("scalar", TS, t_mm)
                    wait("scalar", VQ, moe_state["g_free_v"])
                    t_g = inc("scalar", SQ, lambda s, pq=p_: s.activation(
                        g_buf[:, 0:512], P2[pq][:, 0:512], AF.Relu, bias=0.0, scale=1.0))
                    pair_free[p_] = (SQ, t_g)
                    wait("vector", SQ, t_g)
                    moe_state["g_free_v"] = inc("vector", VQ, lambda v, f2=fst: v.tensor_tensor(
                        sh_ap(f2, 0, 512), g_buf[:, 0:512], g_buf[:, 0:512], ALU.mult))
                if tc == 0:
                    emit_cnt_mms()
                    emit_counts_chain()
                elif tc == 1:
                    emit_broadcasts()
                    emit_dest_chain()
                elif tc == 2:
                    for i in range(NT):
                        emit_xtrans(i)
                sh_v = VQ.n
                wait("tensor", VQ, sh_v)
                wait("tensor", ws2a[0], ws2a[1])
                wait("tensor", ws2b[0], ws2b[1])
                for tt in range(4):
                    ttile = tc * 4 + tt
                    p_ = 2 if tc < 2 else 2 + tt % 2
                    pair_wait(p_)
                    for fst in range(NFS):
                        st_ap = sh_ap(fst, tt * 128, 128)
                        w2s = slot(wsl, fst // 8)
                        fo = fst % 8
                        for ci in range(2):
                            if fst == NFS - 1 and ci == 1:
                                break
                            em("tensor", lambda t, a=st_ap, w=w2s, f2=fo, c2=ci, pq=p_,
                               fi=(fst == 0), la=(fst == NFS - 1):
                               t.matmul(P2[pq][:, c2 * 512:c2 * 512 + 512], a,
                                        w[:, f2 * D + c2 * 512:f2 * D + c2 * 512 + 512],
                                        start=fi, stop=la))
                    st_ap = sh_ap(NFS - 1, tt * 128, 128)
                    w2s = slot(wsl, 1)
                    t_mm = inc("tensor", TS, lambda t, a=st_ap, w=w2s, pq=p_: t.matmul(
                        P2[pq][:, 512:1024], a, w[:, 7 * D + 512:7 * D + 1024], start=False, stop=True))
                    wait("vector", TS, t_mm)
                    wait("vector", SPL, spill_done)
                    dst = x_sb[:, ttile * D:ttile * D + D]
                    tv = inc("vector", VQ, lambda v, d=dst, pq=p_: v.tensor_copy(d, P2[pq][:]))
                    pair_free[p_] = (VQ, tv)
            xsc_barrier = XSC.n

            ckpt()
            sh_all_ts = TS.n
            slot_h1_ts = {}
            slot_y_ts = {}
            trans_ts = {-1: 0}
            tmr_val = {}
            ywr_prev = [YWR.n, YWR.n]
            wrdy = {}

            def gather_w(s):
                h = s % 2
                sem1 = LCA if h == 0 else LCB
                sem2 = LWA if h == 0 else LWB
                aft1 = (TS, slot_h1_ts[s - 2]) if s >= 2 else (TS, (sh_h0_ts if h == 0 else sh_h1_ts))
                aft2 = (TS, slot_y_ts[s - 2]) if s >= 2 else (TS, sh_all_ts)
                out = []
                for (sem, dst_t, src, aft) in ((sem1, r_c, w1s_h, aft1), (sem2, wsl, w2s_h, aft2)):
                    wait("gpsimd", sem, sem.n)
                    wait("gpsimd", WCS, wcs_done)
                    wait("gpsimd", VQ, cstate["guard_v"])
                    wait("gpsimd", aft[0], aft[1])
                    em("gpsimd", lambda g, d=slot(dst_t, h), s2=src, c=widxi[:, s:s + 1], ss=sem:
                       g.indirect_dma_start(out=d, out_offset=None, in_=s2,
                                            in_offset=IndirectOffsetOnAxis(ap=c, axis=0)).then_inc(ss.h, 16))
                    sem.n += 16
                    out.append((sem, sem.n))
                return out

            def read_tm(s):
                wait("sync", XSC, xsc_barrier)
                wait("sync", TMR, TMR.n)
                wait("sync", TS, trans_ts[s - 1])
                for i in range(2):
                    em("sync", lambda sy, d=tmv(i), r0=s * SLOT + i * 128:
                       sy.dma_start(out=d, in_=xnc_h[r0:r0 + 128, :]).then_inc(TMR.h, 16))
                    TMR.n += 16
                tmr_val[s] = TMR.n

            wrdy[0] = gather_w(0)
            read_tm(0)
            wrdy[1] = gather_w(1)
            for s in range(NSLOT):
                h = s % 2
                wait("tensor", TMR, tmr_val[s])
                pair_wait(0)
                pair_wait(1)
                pviews = [P2[0][:].bitcast(BF16), P2[1][:].bitcast(BF16)]
                for i in range(2):
                    for k in range(KD):
                        j = i * KD + k
                        fn = lambda t, s2=r_b[:, i * 1024 + k * 128:i * 1024 + k * 128 + 128], \
                                pv=pviews[j // 8], cc=pc8(j % 8): t.transpose(pv[:, cc:cc + 128], s2, idb[:])
                        if j == 15:
                            t_tr = inc("tensor", TS, fn)
                        else:
                            em("tensor", fn)
                trans_ts[s] = t_tr
                wait("vector", TS, t_tr)
                wait("scalar", TS, t_tr)
                if s >= 1:
                    wait("vector", TS, slot_h1_ts[s - 1])
                    wait("scalar", TS, slot_h1_ts[s - 1])
                tv_x = ts_x = 0
                for i in range(2):
                    for k in range(KD):
                        j = i * KD + k
                        dst = xnsv(k)[:, i * 128:(i + 1) * 128]
                        if j < 8:
                            fn_v = lambda v, d=dst, pv=pviews[0], cc=pc8(j % 8): v.tensor_copy(d, pv[:, cc:cc + 128])
                            if j == 7:
                                tv_x = inc("vector", VQ, fn_v)
                            else:
                                em("vector", fn_v)
                        else:
                            fn_s = lambda sc, d=dst, pv=pviews[1], cc=pc8(j % 8): sc.copy(d, pv[:, cc:cc + 128])
                            if j == 15:
                                ts_x = inc("scalar", SQ, fn_s)
                            else:
                                em("scalar", fn_s)
                pair_free[0] = (VQ, tv_x)
                pair_free[1] = (SQ, ts_x)
                if s + 1 < NSLOT:
                    read_tm(s + 1)
                wait("tensor", wrdy[s][0][0], wrdy[s][0][1])
                wait("tensor", VQ, tv_x)
                wait("tensor", SQ, ts_x)
                sq_v = [0] * KD
                for ft in range(KD):
                    pq = 2 + ft % 2
                    pair_wait(pq)
                    for k in range(KD):
                        st_ap = slot(r_c, h)[:, k * D + ft * 128:k * D + ft * 128 + 128]
                        fn = lambda t, a=st_ap, kk=k, pq2=pq: t.matmul(
                            P2[pq2][:, 0:SLOT], a, xnsv(kk), start=(kk == 0), stop=(kk == KD - 1))
                        if k == KD - 1:
                            t_mm = inc("tensor", TS, fn)
                        else:
                            em("tensor", fn)
                    wait("scalar", TS, t_mm)
                    wait("scalar", VQ, moe_state["g_free_v"])
                    t_g = inc("scalar", SQ, lambda s2, pq2=pq: s2.activation(
                        g_buf[:, 0:SLOT], P2[pq2][:, 0:SLOT], AF.Relu, bias=0.0, scale=1.0))
                    pair_free[pq] = (SQ, t_g)
                    wait("vector", SQ, t_g)
                    if s >= 1 and ft == 0:
                        wait("vector", TS, slot_y_ts[s - 1])
                    moe_state["g_free_v"] = inc("vector", VQ, lambda v, f2=ft: v.tensor_tensor(
                        h1sv(f2), g_buf[:, 0:SLOT], g_buf[:, 0:SLOT], ALU.mult))
                    sq_v[ft] = moe_state["g_free_v"]
                slot_h1_ts[s] = TS.n
                h1_v = VQ.n
                wait("tensor", wrdy[s][1][0], wrdy[s][1][1])
                for tt in range(2):
                    pq = 2 + tt
                    pair_wait(pq)
                    for ft in range(KD):
                        wait("tensor", VQ, sq_v[ft])
                        st_ap = h1sv(ft)[:, tt * 128:tt * 128 + 128]
                        for ci in range(2):
                            fn = lambda t, a=st_ap, kk=ft, c2=ci, pq2=pq, hh=h: t.matmul(
                                P2[pq2][:, c2 * 512:c2 * 512 + 512], a,
                                slot(wsl, hh)[:, kk * D + c2 * 512:kk * D + c2 * 512 + 512],
                                start=(kk == 0), stop=(kk == KD - 1))
                            if ft == KD - 1 and ci == 1:
                                t_mm = inc("tensor", TS, fn)
                            else:
                                em("tensor", fn)
                    wait("vector", TS, t_mm)
                    wait("vector", YWR, ywr_prev[tt])
                    tv = inc("vector", VQ, lambda v, d=ystv(tt), pq2=pq: v.tensor_copy(d, P2[pq2][:]))
                    pair_free[pq] = (VQ, tv)
                    wait("sync", VQ, tv)
                    em("sync", lambda sy, d=ystv(tt), r0=s * SLOT + tt * 128:
                       sy.dma_start(out=yc_h[r0:r0 + 128, :], in_=d).then_inc(YWR.h, 16))
                    YWR.n += 16
                    ywr_prev[tt] = YWR.n
                slot_y_ts[s] = TS.n
                if s + 2 < NSLOT:
                    wrdy[s + 2] = gather_w(s + 2)
            ywr_done = YWR.n

            ckpt()
            if b + 1 < B_LOC:
                wq_rdy = load_w(wsl, 0, wq_h, after=(TS, slot_y_ts[NSLOT - 1]))
                r_cf = r_c[:].bitcast(F32)
                rcx = lambda t: r_cf[:, (t % 8) * D:((t % 8) + 1) * D]
                rcx_rdy = {}
                wait("sync", TS, slot_h1_ts[NSLOT - 1])
                wait("sync", LCA, LCA.n)
                wait("sync", LCB, LCB.n)
                for t_i in range(8):
                    em("sync", lambda s, d=rcx(t_i), r0=(b + 1) * T + t_i * 128:
                       s.dma_start(out=d, in_=x_h[r0:r0 + 128, :]).then_inc(LDX.h, 16))
                    LDX.n += 16
                    rcx_rdy[t_i] = (LDX, LDX.n)
                ln1_h0 = emit_ln_phase(list(range(8)), r_a, aeng="scalar",
                                       src=rcx, src_rdy=lambda t: rcx_rdy[t])
                wkv_rdy = load_w(wsl, 1, wkv_h, after=(TS, slot_y_ts[NSLOT - 1]))

            gfree = [0] * 6

            def emit_recombine(tiles):
                for t_i in tiles:
                    b1 = (2 * t_i) % 6
                    b2 = (2 * t_i + 1) % 6
                    wait("gpsimd", YWR, ywr_done)
                    wait("gpsimd", VQ, cstate["guard_v"])
                    for (bj, di_) in ((b1, d1i), (b2, d2i)):
                        wait("gpsimd", VQ, gfree[bj])
                        em("gpsimd", lambda g, d=gbv(bj), c=di_[:, t_i:t_i + 1]:
                           g.indirect_dma_start(out=d, out_offset=None, in_=yc_h,
                                                in_offset=IndirectOffsetOnAxis(ap=c, axis=0)).then_inc(YGQ.h, 16))
                        YGQ.n += 16
                    ygv = YGQ.n
                    dst = x_sb[:, t_i * D:(t_i + 1) * D]
                    wait("vector", YGQ, ygv)
                    t = inc("vector", VQ, lambda v, d=dst, g1=gbv(b1), w=w1tok[:, t_i:t_i + 1]:
                            v.scalar_tensor_tensor(d, g1, w, d, ALU.mult, ALU.add))
                    wait("vector", VQ, t)
                    t2 = inc("vector", VQ, lambda v, d=dst, g2=gbv(b2), w=w2tok[:, t_i:t_i + 1]:
                             v.scalar_tensor_tensor(d, g2, w, d, ALU.mult, ALU.add))
                    wait("vector", VQ, t2)
                    gfree[b1] = t2
                    gfree[b2] = t2

            emit_recombine(range(0, 4))
            emit_final(0)
            emit_recombine(range(4, 8))
            emit_final(1)
            if b + 1 < B_LOC:
                g_ts = slot_y_ts[NSLOT - 1]
                gsems = ((LCA, LCA.n), (LCB, LCB.n), (YWR, YWR.n))
                emit_proj_half(0, r_b, SCALE, 0, ln1_h0[0], ln1_h0[1], wq_rdy,
                               guard_ts=g_ts, guard_sems=gsems, evac_eng="scalar")
                wait("sync", VQ, ln1_h0[0])
                wait("sync", SQ, ln1_h0[1])
                for t_i in range(8, NT):
                    em("sync", lambda s, d=rcx(t_i), r0=(b + 1) * T + t_i * 128:
                       s.dma_start(out=d, in_=x_h[r0:r0 + 128, :]).then_inc(LDX.h, 16))
                    LDX.n += 16
                    rcx_rdy[t_i] = (LDX, LDX.n)
                ln1_h1 = emit_ln_phase(list(range(8, NT)), r_a, aeng="scalar",
                                       src=rcx, src_rdy=lambda t: rcx_rdy[t])
            emit_recombine(range(8, 12))
            emit_final(2)
            if b + 1 < B_LOC:
                emit_proj_half(1, r_c, 1.0, 0, ln1_h0[0], ln1_h0[1], wkv_rdy,
                               guard_sems=((VQ, ln1_h1[0]), (SQ, ln1_h1[1])),
                               evac_eng="scalar")
            emit_recombine(range(12, NT))
            emit_final(3)

            ckpt()
            if b + 1 < B_LOC:
                emit_proj_half(0, r_b, SCALE, 1, ln1_h1[0], ln1_h1[1], wq_rdy)
                emit_proj_half(1, r_c, 1.0, 1, ln1_h1[0], ln1_h1[1], wkv_rdy)
                carry = {"qkv_ts": TS.n, "kv_evac_v": VQ.n, "kv_evac_s": SQ.n}
                x_ready[b + 1] = (LDX, LDX.n)
        except _StopEmission:
            if os.environ.get("MK_DUMP"):
                wait("sync", VQ, VQ.n)
                wait("sync", SQ, SQ.n)
                wait("sync", TS, TS.n)
                wait("sync", LDX, LDX.n)
                wait("sync", GAA, GAA.n)
                wait("sync", GAB, GAB.n)
                em("sync", lambda s: s.dma_start(
                    out=out_h[0:T, :].rearrange("(n p) d -> p n d", p=128),
                    in_=x_sb[:].rearrange("p (n d) -> p n d", n=NT)).then_inc(STA.h, 16))
                STA.n += 16

        for s_ in (STA, STB, LDX, SPL, XSC, WCS, TRV, TRW, YWR, YGQ, TMR, GAA, GAB, LWA, LWB, LCA, LCB, LRT, TS, VQ, SQ):
            em("gpsimd", lambda g, ss=s_: g.wait_ge(ss.h, ss.n))

        with nc.Block("mk", no_gpsimd_drain=True) as blk:
            blk.tensor(lambda e: [f(e) for f in P.prog["tensor"]])
            blk.vector(lambda e: [f(e) for f in P.prog["vector"]])
            blk.scalar(lambda e: [f(e) for f in P.prog["scalar"]])
            blk.sync(lambda e: [f(e) for f in P.prog["sync"]])
            blk.gpsimd(lambda e: [f(e) for f in P.prog["gpsimd"]])

    return nc


_NC_CACHE = None


def kernel(**inputs):
    global _NC_CACHE
    x = np.asarray(inputs["L0_attn_hidden_states"], dtype=np.float32)
    wq = np.ascontiguousarray(np.asarray(inputs["L0_attn_Wq"], dtype=np.float32))
    wkv = np.ascontiguousarray(np.asarray(inputs["L0_attn_Wkv"], dtype=np.float32))
    wout = np.ascontiguousarray(np.asarray(inputs["L0_attn_Wout"], dtype=np.float32))
    bout = np.ascontiguousarray(np.asarray(inputs["L0_attn_bout"], dtype=np.float32).reshape(1, D))
    rt = np.ascontiguousarray(np.asarray(inputs["L0_ffn_w_router"], dtype=np.float32).T)
    w1t = np.ascontiguousarray(np.asarray(inputs["L0_ffn_w1_stack"], dtype=np.float32).transpose(0, 2, 1))
    w2t = np.ascontiguousarray(np.asarray(inputs["L0_ffn_w2_stack"], dtype=np.float32).transpose(0, 2, 1))
    ws1t = np.ascontiguousarray(np.asarray(inputs["L0_ffn_ws1"], dtype=np.float32).T)
    ws2t = np.ascontiguousarray(np.asarray(inputs["L0_ffn_ws2"], dtype=np.float32).T)
    ident = np.eye(128, dtype=np.float32)
    ltri = np.ascontiguousarray(np.triu(np.ones((128, 128), dtype=np.float32), 1))
    cstr = np.zeros((1, 64), dtype=np.float32)
    cstr[0, 0:16] = np.arange(16, dtype=np.float32) * SLOT + 0.5
    cstr[0, 32:32 + NSLOT] = np.arange(NSLOT, dtype=np.float32) * SLOT + 0.5
    iotap = np.zeros((128, 2), dtype=np.float32)
    iotap[:, 0] = np.arange(128, dtype=np.float32)
    iotap[:, 1] = 1.0

    if _NC_CACHE is None:
        _NC_CACHE = build_program()
    nc = _NC_CACHE
    in_maps = []
    for c in range(NCORE):
        shard = np.ascontiguousarray(x[c * B_LOC:(c + 1) * B_LOC].reshape(B_LOC * T, D))
        in_maps.append({"x": shard, "wq": wq, "wkv": wkv, "wout": wout, "boutr": bout,
                        "rt": rt, "w1t": w1t, "w2t": w2t, "ws1t": ws1t, "ws2t": ws2t,
                        "ident": ident, "ltri": ltri, "cstr": cstr, "iotap": iotap})
    res = run_bass_kernel_spmd(nc, in_maps, core_ids=list(range(NCORE)))
    outs = [np.asarray(r["out"]).reshape(B_LOC, T, D) for r in res.results]
    return np.concatenate(outs, axis=0)


# version 35 (final)
# speedup vs baseline: 1.0233x; 1.0233x over previous
import os
import numpy as np
from contextlib import ExitStack

import concourse.bass as bass
import concourse.mybir as mybir
from concourse.bass import IndirectOffsetOnAxis
from concourse.bass_utils import run_bass_kernel_spmd

F32 = mybir.dt.float32
BF16 = mybir.dt.bfloat16
AF = mybir.ActivationFunctionType
ALU = mybir.AluOpType
AX = mybir.AxisListType

B_LOC = 4
T = 2048
D = 1024
NT = T // 128
KD = D // 128
H = 16
E = 8
FS = 2048
NFS = FS // 128
NCORE = 8
SCALE = 0.125
SLOT = 256
NSLOT = 23
NCAP = NSLOT * SLOT


class Sem:
    def __init__(self, handle):
        self.h = handle
        self.n = 0


class Plan:
    ENGS = ("tensor", "vector", "scalar", "sync", "gpsimd")

    def __init__(self):
        self.prog = {e: [] for e in self.ENGS}
        self.last_wait = {e: {} for e in self.ENGS}

    def em(self, eng, fn):
        self.prog[eng].append(fn)

    def wait(self, eng, sem, val):
        if sem is None or val <= 0:
            return
        lw = self.last_wait[eng]
        if lw.get(id(sem), -1) >= val:
            return
        lw[id(sem)] = val
        self.em(eng, lambda e, s=sem.h, v=val: e.wait_ge(s, v))

    def inc(self, eng, sem, fn, k=1):
        sem.n += k
        self.em(eng, lambda e, f=fn, s=sem.h, kk=k: f(e).then_inc(s, kk))
        return sem.n


class _StopEmission(Exception):
    pass


def build_program():
    limit = int(os.environ.get("MK_LIMIT", "99999"))
    _ph = [0]

    def ckpt():
        _ph[0] += 1
        if _ph[0] > limit:
            raise _StopEmission()

    nc = bass.Bass("TRN2", target_bir_lowering=False)

    x_h = nc.dram_tensor("x", [B_LOC * T, D], F32, kind="ExternalInput").ap()
    out_h = nc.dram_tensor("out", [B_LOC * T, D], F32, kind="ExternalOutput").ap()
    wq_h = nc.dram_tensor("wq", [D, D], F32, kind="ExternalInput").ap()
    wkv_h = nc.dram_tensor("wkv", [D, D], F32, kind="ExternalInput").ap()
    wout_h = nc.dram_tensor("wout", [D, D], F32, kind="ExternalInput").ap()
    bout_h = nc.dram_tensor("boutr", [1, D], F32, kind="ExternalInput").ap()
    rt_h = nc.dram_tensor("rt", [D, E], F32, kind="ExternalInput").ap()
    id_h = nc.dram_tensor("ident", [128, 128], F32, kind="ExternalInput").ap()
    w1t_h = nc.dram_tensor("w1t", [E, D, D], F32, kind="ExternalInput").ap()
    w2t_h = nc.dram_tensor("w2t", [E, D, D], F32, kind="ExternalInput").ap()
    ws1t_h = nc.dram_tensor("ws1t", [D, FS], F32, kind="ExternalInput").ap()
    ws2t_h = nc.dram_tensor("ws2t", [FS, D], F32, kind="ExternalInput").ap()
    spill_h = nc.dram_tensor("xspill", [T, D], F32, kind="Internal").ap()
    ltri_h = nc.dram_tensor("ltri", [128, 128], F32, kind="ExternalInput").ap()
    cst_h = nc.dram_tensor("cstr", [1, 64], F32, kind="ExternalInput").ap()
    iop_h = nc.dram_tensor("iotap", [128, 2], F32, kind="ExternalInput").ap()
    w1s_h = nc.dram_tensor("w1s", [E * 128, KD * D], BF16, kind="Internal").ap()
    w2s_h = nc.dram_tensor("w2s", [E * 128, KD * D], BF16, kind="Internal").ap()
    xnc_h = nc.dram_tensor("xnc", [NCAP, D], BF16, kind="Internal").ap()
    yc_h = nc.dram_tensor("yc", [NCAP, D], BF16, kind="Internal").ap()

    stack = ExitStack()
    sb = lambda n, s, d: stack.enter_context(nc.sbuf_tensor(n, s, d))
    pst = lambda n, s, d: stack.enter_context(nc.psum_tensor(n, s, d))
    mksem = lambda n: Sem(stack.enter_context(nc.semaphore(n)))

    with stack:
        x_sb = sb("x_sb", [128, NT * D], F32)
        r_a = sb("r_a", [128, KD * T], BF16)
        r_b = sb("r_b", [128, KD * T], BF16)
        r_c = sb("r_c", [128, KD * T], BF16)
        wsl = sb("wsl", [128, KD * T], BF16)
        att0 = sb("att0", [128, T], BF16)
        att1 = sb("att1", [128, T], BF16)
        g_buf = sb("g_buf", [128, 512], BF16)
        bout_r = sb("bout_r", [1, D], BF16)
        ones1 = sb("ones1", [1, 128], BF16)
        rt_sb = sb("rt_sb", [128, KD * E], BF16)
        epsv = sb("epsv", [128, 1], F32)
        stA = sb("stA", [128, 192], F32)
        st2A = sb("st2A", [128, 32], F32)
        lnvA = sb("lnvA", [128, 16], F32)
        rrA = sb("rrA", [128, 16], F32)
        nmrA = sb("nmrA", [128, 16], F32)
        st12 = sb("st12", [128, 12], F32)
        st2 = sb("st2", [128, 2], F32)
        lnv = sb("lnv", [128, 1], F32)
        rr = sb("rr", [128, 1], F32)
        nmr = sb("nmr", [128, 1], F32)
        wts = sb("wts", [128, NT * E], F32)
        lgA = sb("lgA", [128, NT * E], F32)
        bufA = sb("bufA", [128, NT * E], F32)
        m1g = sb("m1g", [128, NT], F32)
        m2g = sb("m2g", [128, NT], F32)
        deng = sb("deng", [128, NT], F32)
        rdeng = sb("rdeng", [128, NT], F32)
        idb = sb("idb", [128, 128], BF16)
        bufB = stA[:, 0:128]
        ltb = sb("ltb", [128, 128], BF16)
        iop = sb("iop_sb", [128, 2], F32)
        onecb = sb("onecb", [128, 1], BF16)
        cstr = sb("cstr_sb", [1, 64], F32)
        w1tok = sb("w1tok", [128, NT], F32)
        w2tok = sb("w2tok", [128, NT], F32)
        d1i = sb("d1i", [128, NT], mybir.dt.int32)
        d2i = sb("d2i", [128, NT], mybir.dt.int32)
        widxi = sb("widxi", [128, 32], mybir.dt.int32)
        iguard = sb("iguard", [128, 1], mybir.dt.int32)
        att1f = att1[:].bitcast(F32)
        m1s = att1f[:, 0:128]
        posf = att1f[:, 128:256]
        dif = att1f[:, 256:384]
        mskb = att1[:, 768:896]
        ctr = att1f[0:1, 448:576]
        cta = att1f[0:1, 576:704]
        tbA = att1f[0:1, 704:832]
        tbB = att1f[0:1, 832:960]
        pA8 = att1f[0:1, 960:968]
        pB8 = att1f[0:1, 968:976]
        reg8 = att1f[0:1, 976:984]
        sexpr = att1f[0:1, 984:1016]
        rowb = stA[0:1, 0:192]
        ones1f = stA[0:1, 64:192]
        widxf = stA[:, 128:160]
        dst1f = stA[:, 160:176]
        dst2f = stA[:, 176:192]

        yt0 = att0[:].bitcast(F32)
        yt1 = att1[:].bitcast(F32)
        xn_tmp0 = att0[:, 0:D]
        xn_tmp1 = att0[:, D:2 * D]

        P2 = [pst(f"p2_{i}", [128, 1024], F32) for i in range(4)]

        LWA = mksem("lwa")
        LWB = mksem("lwb")
        LCA = mksem("lca")
        LCB = mksem("lcb")
        LRT = mksem("lrt")
        LDX = mksem("ldx")
        XSC = mksem("xsc")
        WCS = mksem("wcs")
        TRV = mksem("trv")
        TRW = mksem("trw")
        YWR = mksem("ywr")
        YGQ = mksem("ygq")
        TMR = mksem("tmr")
        SPL = mksem("spl")
        STA = mksem("sta")
        STB = mksem("stb")
        GAA = mksem("gaa")
        GAB = mksem("gab")
        TS = mksem("ts")
        VQ = mksem("vq")
        SQ = mksem("sq")

        P = Plan()
        em, wait, inc = P.em, P.wait, P.inc

        pair_free = [None] * 4

        def pair_wait(i):
            if pair_free[i] is not None:
                deps = pair_free[i]
                if not isinstance(deps, list):
                    deps = [deps]
                for (sm, vv) in deps:
                    wait("tensor", sm, vv)

        def slot(tensor, s):
            return tensor[:, s * KD * D:(s + 1) * KD * D]

        def fmv(t_, k, c0, n):
            return t_[:, k * T + c0:k * T + c0 + n]

        def h1q_ap(half, ft, j0, n):
            return r_b[:, half * 8192 + ft * 1024 + j0:half * 8192 + ft * 1024 + j0 + n]

        def sh_ap(fst, j0, n):
            return r_b[:, KD * 1024 + fst * 512 + j0:KD * 1024 + fst * 512 + j0 + n]

        em("vector", lambda v: v.memset(ones1[:], 1.0))
        inc("vector", VQ, lambda v: v.memset(epsv[:], 1e-5))

        em("gpsimd", lambda g: g.dma_start(out=bout_r[:], in_=bout_h).then_inc(LRT.h, 16))
        LRT.n += 16
        em("gpsimd", lambda g: g.dma_start(
            out=rt_sb[:].rearrange("p (k e) -> p k e", k=KD),
            in_=rt_h.rearrange("(k p) e -> p k e", p=128)).then_inc(LRT.h, 16))
        LRT.n += 16
        em("gpsimd", lambda g: g.dma_start(out=idb[:], in_=id_h).then_inc(LRT.h, 16))
        LRT.n += 16
        em("gpsimd", lambda g: g.dma_start(out=ltb[:], in_=ltri_h).then_inc(LRT.h, 16))
        LRT.n += 16
        em("sync", lambda s: s.dma_start(out=cstr[:], in_=cst_h).then_inc(LRT.h, 16))
        LRT.n += 16
        em("sync", lambda s: s.dma_start(out=iop[:], in_=iop_h).then_inc(LRT.h, 16))
        LRT.n += 16
        rt_rdy = (LRT, LRT.n)
        bout_rdy = (LRT, LRT.n)
        id_rdy = (LRT, LRT.n)
        cst_rdy = (LRT, LRT.n)
        wait("vector", LRT, LRT.n)
        t_onec = inc("vector", VQ, lambda v: v.tensor_copy(onecb[:], iop[:, 1:2]))

        em("sync", lambda s: s.dma_start(
            out=x_sb[:].rearrange("p (n d) -> p n d", n=NT),
            in_=x_h[0:T, :].rearrange("(n p) d -> p n d", p=128)).then_inc(LDX.h, 16))
        LDX.n += 16
        x_ready = [(LDX, LDX.n)] + [None] * (B_LOC - 1)

        def load_w(dst_tensor, s, src_ap, after):
            sem = (LWA if s == 0 else LWB) if dst_tensor is wsl else (LCA if s == 0 else LCB)
            wait("gpsimd", sem, sem.n)
            if after is not None:
                wait("gpsimd", after[0], after[1])
            dst = slot(dst_tensor, s).rearrange("p (k n) -> p k n", k=KD)
            src = src_ap.rearrange("(k p) n -> p k n", p=128)
            em("gpsimd", lambda g, d=dst, sr=src, ss=sem: g.dma_start(out=d, in_=sr).then_inc(ss.h, 16))
            sem.n += 16
            return (sem, sem.n)

        wq_rdy = load_w(wsl, 0, wq_h, None)
        wkv_rdy = load_w(wsl, 1, wkv_h, None)

        for e in range(E):
            for (srct, dstt) in ((w1t_h, w1s_h), (w2t_h, w2s_h)):
                em("gpsimd", lambda g, s_=srct[e], d_=dstt[e * 128:(e + 1) * 128, :]:
                   g.dma_start(out=d_.rearrange("p (k n) -> p k n", k=KD),
                               in_=s_.rearrange("(k p) n -> p k n", p=128)).then_inc(WCS.h, 16))
                WCS.n += 16
        wcs_done = WCS.n

        xn_tmp_free = [None, None]
        yt_store_spq = [None, None]
        x_tile_rdy = [None] * NT

        def emit_ln_phase(tiles, dst_region, aeng="vector", src=None, src_rdy=None):
            if src is None:
                src = lambda t: x_sb[:, t * D:t * D + D]
            if src_rdy is None:
                src_rdy = lambda t: x_tile_rdy[t]
            asem = VQ if aeng == "vector" else SQ
            wait("vector", VQ, VQ.n)
            if yt_store_spq[0] is not None:
                wait(aeng, yt_store_spq[0][0], yt_store_spq[0][1])
            for i, t_i in enumerate(tiles):
                if src_rdy(t_i) is not None:
                    wait("vector", src_rdy(t_i)[0], src_rdy(t_i)[1])
                sl = src(t_i)
                em("vector", lambda v, s=sl, ii=i: v.bn_stats(stA[:, ii * 12:ii * 12 + 6], s[:, 0:512]))
                em("vector", lambda v, s=sl, ii=i: v.bn_stats(stA[:, ii * 12 + 6:ii * 12 + 12], s[:, 512:1024]))
            t = inc("vector", VQ, lambda v: v.engine_nop())
            wait("vector", VQ, t)
            n = len(tiles)
            for i in range(n):
                em("vector", lambda v, ii=i: v.bn_aggr(st2A[:, ii * 2:ii * 2 + 2], stA[:, ii * 12:ii * 12 + 12]))
            t_ag = inc("vector", VQ, lambda v: v.engine_nop())
            wait("scalar", VQ, t_ag)
            t_ln = inc("scalar", SQ, lambda s, nn=n: s.activation(
                lnvA[:, 0:nn], st2A[:, 1:2 * nn:2], AF.Ln, bias=epsv[:], scale=1.0))
            wait("scalar", SQ, t_ln)
            t_rr = inc("scalar", SQ, lambda s, nn=n: s.activation(
                rrA[:, 0:nn], lnvA[:, 0:nn], AF.Exp, bias=0.0, scale=-0.5))
            wait("vector", SQ, t_rr)
            t_nm = inc("vector", VQ, lambda v, nn=n: v.scalar_tensor_tensor(
                nmrA[:, 0:nn], st2A[:, 0:2 * nn:2], -1.0, rrA[:, 0:nn], ALU.mult, ALU.mult))
            wait("vector", VQ, t_nm)
            wait("tensor", id_rdy[0], id_rdy[1])
            wait(aeng, VQ, t_nm)
            for i, t_i in enumerate(tiles):
                par = t_i % 2
                tmp = xn_tmp0 if par == 0 else xn_tmp1
                if xn_tmp_free[par] is not None:
                    wait(aeng, xn_tmp_free[par][0], xn_tmp_free[par][1])
                if aeng == "vector":
                    t_ap = inc("vector", VQ, lambda v, s=src(t_i), d=tmp, ii=i:
                               v.tensor_scalar(d, s, rrA[:, ii:ii + 1], nmrA[:, ii:ii + 1], ALU.mult, ALU.add))
                else:
                    t_ap = inc("scalar", SQ, lambda s2, s=src(t_i), d=tmp, ii=i:
                               s2.activation(d, s, AF.Identity, bias=nmrA[:, ii:ii + 1], scale=rrA[:, ii:ii + 1]))
                pp = par
                pair_wait(pp)
                wait("tensor", asem, t_ap)
                pview = P2[pp][:].bitcast(BF16)
                pcol = lambda kk: kk * 128 if kk < 4 else 1024 + (kk - 4) * 128
                for k in range(3):
                    em("tensor", lambda t, tm=tmp, kk=k, pv=pview: t.transpose(
                        pv[:, pcol(kk):pcol(kk) + 128], tm[:, kk * 128:kk * 128 + 128], idb[:]))
                t_trA = inc("tensor", TS, lambda t, tm=tmp, pv=pview: t.transpose(
                    pv[:, pcol(3):pcol(3) + 128], tm[:, 3 * 128:4 * 128], idb[:]))
                for k in range(4, KD - 1):
                    em("tensor", lambda t, tm=tmp, kk=k, pv=pview: t.transpose(
                        pv[:, pcol(kk):pcol(kk) + 128], tm[:, kk * 128:kk * 128 + 128], idb[:]))
                t_tr = inc("tensor", TS, lambda t, tm=tmp, pv=pview: t.transpose(
                    pv[:, pcol(KD - 1):pcol(KD - 1) + 128], tm[:, (KD - 1) * 128:KD * 128], idb[:]))
                xn_tmp_free[par] = (TS, t_tr)
                wait("vector", TS, t_trA)
                wait("scalar", TS, t_tr)
                for k in range(3):
                    em("vector", lambda v, kk=k, tt=t_i, dr=dst_region, pv=pview: v.tensor_copy(
                        fmv(dr, kk, tt * 128, 128), pv[:, pcol(kk):pcol(kk) + 128]))
                tv = inc("vector", VQ, lambda v, tt=t_i, dr=dst_region, pv=pview: v.tensor_copy(
                    fmv(dr, 3, tt * 128, 128), pv[:, pcol(3):pcol(3) + 128]))
                for k in range(4, KD - 1):
                    em("scalar", lambda s2, kk=k, tt=t_i, dr=dst_region, pv=pview: s2.copy(
                        fmv(dr, kk, tt * 128, 128), pv[:, pcol(kk):pcol(kk) + 128]))
                ts_ = inc("scalar", SQ, lambda s2, tt=t_i, dr=dst_region, pv=pview: s2.copy(
                    fmv(dr, KD - 1, tt * 128, 128), pv[:, pcol(KD - 1):pcol(KD - 1) + 128]))
                pair_free[pp] = [(VQ, tv), (SQ, ts_)]
            return (VQ.n, SQ.n)
        def emit_proj_half(wslot_i, dst_region, scale, half, ln_v, ln_s, w_rdy,
                           guard_ts=None, guard_sems=(), evac_eng="split"):
            wait("tensor", VQ, ln_v)
            wait("tensor", SQ, ln_s)
            wait("tensor", w_rdy[0], w_rdy[1])
            tok0 = half * 1024
            first_evac = [True]
            for dt in range(KD):
                p_ = dt % 2
                pair_wait(p_)
                for k in range(KD):
                    st_ap = slot(wsl, wslot_i)[:, k * D + dt * 128:k * D + dt * 128 + 128]
                    for ci in range(2):
                        fn = lambda t, a=st_ap, kk=k, c2=ci, pq=p_, t0=tok0: t.matmul(
                            P2[pq][:, c2 * 512:c2 * 512 + 512], a,
                            fmv(r_a, kk, t0 + c2 * 512, 512),
                            start=(kk == 0), stop=(kk == KD - 1))
                        if k == KD - 1 and ci == 1:
                            t_mm = inc("tensor", TS, fn)
                        else:
                            em("tensor", fn)
                if evac_eng == "scalar":
                    eng, sem = "scalar", SQ
                else:
                    eng = "vector" if dt % 2 == 0 else "scalar"
                    sem = VQ if dt % 2 == 0 else SQ
                wait(eng, TS, t_mm)
                if first_evac[0]:
                    first_evac[0] = False
                    if guard_ts is not None:
                        wait("vector", TS, guard_ts)
                        wait("scalar", TS, guard_ts)
                    for gs in guard_sems:
                        wait("vector", gs[0], gs[1])
                        wait("scalar", gs[0], gs[1])
                d_ = fmv(dst_region, dt, tok0, 1024)
                if scale == 1.0:
                    if eng == "vector":
                        tval = inc("vector", VQ, lambda v, d=d_, pq=p_: v.tensor_copy(d, P2[pq][:]))
                    else:
                        tval = inc("scalar", SQ, lambda s, d=d_, pq=p_: s.copy(d, P2[pq][:]))
                else:
                    if eng == "vector":
                        tval = inc("vector", VQ, lambda v, d=d_, pq=p_, sc=scale:
                                   v.tensor_scalar(d, P2[pq][:], sc, None, ALU.mult))
                    else:
                        tval = inc("scalar", SQ, lambda s, d=d_, pq=p_, sc=scale:
                                   s.mul(d, P2[pq][:], sc))
                pair_free[p_] = (sem, tval)
        carry = {}
        try:
          for b in range(B_LOC):
            if b == 0:
                wait("vector", x_ready[b][0], x_ready[b][1])
                ln1_rdy = emit_ln_phase(list(range(NT)), r_a)
                ckpt()
                emit_proj_half(0, r_b, SCALE, 0, ln1_rdy[0], ln1_rdy[1], wq_rdy)
                emit_proj_half(1, r_c, 1.0, 0, ln1_rdy[0], ln1_rdy[1], wkv_rdy)
                emit_proj_half(0, r_b, SCALE, 1, ln1_rdy[0], ln1_rdy[1], wq_rdy)
                emit_proj_half(1, r_c, 1.0, 1, ln1_rdy[0], ln1_rdy[1], wkv_rdy)
                qkv_ts = TS.n
                kv_evac_v, kv_evac_s = VQ.n, SQ.n
            else:
                qkv_ts = carry["qkv_ts"]
                kv_evac_v = carry["kv_evac_v"]
                kv_evac_s = carry["kv_evac_s"]

            wout_rdy = load_w(wsl, 0, wout_h, after=(TS, qkv_ts))

            ckpt()
            v_tile_rdy = [None] * NT
            for tq, tsem in (("sync", TRV),):
                wait(tq, tsem, tsem.n)
                wait(tq, VQ, kv_evac_v)
                wait(tq, SQ, kv_evac_s)
                wait(tq, TS, qkv_ts)
            for t_i in range(NT):
                tsem = TRV
                tq = "sync"
                wait(tq, tsem, tsem.n)
                for k in range(KD):
                    em(tq, lambda s, kk=k, tt=t_i, ss=tsem: s.dma_start_transpose(
                        out=r_a[:, tt * D + kk * 128:tt * D + kk * 128 + 128],
                        in_=fmv(r_c, kk, tt * 128, 128)).then_inc(ss.h, 16))
                    tsem.n += 16
                v_tile_rdy[t_i] = (tsem, tsem.n)

            ckpt()
            for par in range(2):
                if yt_store_spq[par] is not None:
                    wait("vector", yt_store_spq[par][0], yt_store_spq[par][1])
                    wait("scalar", yt_store_spq[par][0], yt_store_spq[par][1])
            qk_bank = [None] * 4
            for hp in range(H // 2):
                h0 = 2 * hp
                its = [(si, hc) for si in range(NT) for hc in range(2)]
                prev = None
                for (s_i, hc) in its:
                    if s_i == 0 and hc == 0:
                        if hp == 0:
                            pair_wait(0)
                            pair_wait(1)
                        pair_wait(2)
                        pair_wait(3)
                    kst0 = fmv(r_c, hp, s_i * 128, 128)[0:64, :]
                    kst1 = fmv(r_c, hp, s_i * 128, 128)[64:128, :]
                    qk_t = [0, 0, 0, 0]
                    for ci in range(2):
                        c = 2 * hc + ci
                        q0 = fmv(r_b, hp, c * 512, 512)[0:64, :]
                        q1 = fmv(r_b, hp, c * 512, 512)[64:128, :]
                        if qk_bank[2 * ci] is not None:
                            wait("tensor", qk_bank[2 * ci][0], qk_bank[2 * ci][1])
                        fn0 = lambda t, a=kst0, qq=q0, c2=ci: t.matmul(
                            P2[0][:, c2 * 512:c2 * 512 + 512], a, qq, start=True, stop=True,
                            tile_position=(0, 0))
                        qk_t[2 * ci] = inc("tensor", TS, fn0)
                        if qk_bank[2 * ci + 1] is not None:
                            wait("tensor", qk_bank[2 * ci + 1][0], qk_bank[2 * ci + 1][1])
                        mmfn = lambda t, a=kst1, qq=q1, c2=ci: t.matmul(
                            P2[1][:, c2 * 512:c2 * 512 + 512], a, qq, start=True, stop=True,
                            tile_position=(64, 0))
                        qk_t[2 * ci + 1] = inc("tensor", TS, mmfn)
                    t_sc = qk_t[3]
                    if prev is not None:
                        pv, ps_, p_si, p_hc = prev
                        wait("tensor", VQ, pv)
                        wait("tensor", SQ, ps_)
                        wait("tensor", v_tile_rdy[p_si][0], v_tile_rdy[p_si][1])
                        vst0 = r_a[:, p_si * D + h0 * 64:p_si * D + h0 * 64 + 64]
                        vst1 = r_a[:, p_si * D + h0 * 64 + 64:p_si * D + h0 * 64 + 128]
                        pav = 2 + p_hc
                        for ci in range(2):
                            c = 2 * p_hc + ci
                            em("tensor", lambda t, a=vst0, cc=c, c2=ci, pq=pav, fi=(p_si == 0), la=(p_si == NT - 1):
                               t.matmul(P2[pq][0:64, c2 * 512:c2 * 512 + 512], a,
                                        att0[:, cc * 512:cc * 512 + 512], start=fi, stop=la,
                                        skip_group_check=True))
                            em("tensor", lambda t, a=vst1, cc=c, c2=ci, pq=pav, fi=(p_si == 0), la=(p_si == NT - 1):
                               t.matmul(P2[pq][64:128, c2 * 512:c2 * 512 + 512], a,
                                        att1[:, cc * 512:cc * 512 + 512], start=fi, stop=la,
                                        tile_position=(0, 64), skip_group_check=True))
                    wait("vector", TS, qk_t[0])
                    v1 = inc("vector", VQ, lambda v, o0=hc * 1024:
                             v.tensor_scalar_max(att0[:, o0:o0 + 512], P2[0][:, 0:512], 0.0))
                    wait("scalar", TS, qk_t[1])
                    s1 = inc("scalar", SQ, lambda s2, o0=hc * 1024:
                             s2.activation(att1[:, o0:o0 + 512], P2[1][:, 0:512], AF.Relu, bias=0.0, scale=1.0))
                    wait("vector", TS, qk_t[2])
                    tv = inc("vector", VQ, lambda v, o0=hc * 1024:
                             v.tensor_scalar_max(att0[:, o0 + 512:o0 + 1024], P2[0][:, 512:1024], 0.0))
                    wait("scalar", TS, qk_t[3])
                    ts_ = inc("scalar", SQ, lambda s2, o0=hc * 1024:
                              s2.activation(att1[:, o0 + 512:o0 + 1024], P2[1][:, 512:1024], AF.Relu, bias=0.0, scale=1.0))
                    qk_bank[0] = (VQ, v1)
                    qk_bank[1] = (SQ, s1)
                    qk_bank[2] = (VQ, tv)
                    qk_bank[3] = (SQ, ts_)
                    pair_free[0] = (VQ, tv)
                    pair_free[1] = (SQ, ts_)
                    prev = (tv, ts_, s_i, hc)
                pv, ps_, p_si, p_hc = prev
                wait("tensor", VQ, pv)
                wait("tensor", SQ, ps_)
                wait("tensor", v_tile_rdy[p_si][0], v_tile_rdy[p_si][1])
                vst0 = r_a[:, p_si * D + h0 * 64:p_si * D + h0 * 64 + 64]
                vst1 = r_a[:, p_si * D + h0 * 64 + 64:p_si * D + h0 * 64 + 128]
                pav = 2 + p_hc
                for ci in range(2):
                    c = 2 * p_hc + ci
                    em("tensor", lambda t, a=vst0, cc=c, c2=ci, pq=pav:
                       t.matmul(P2[pq][0:64, c2 * 512:c2 * 512 + 512], a,
                                att0[:, cc * 512:cc * 512 + 512], start=False, stop=True,
                                skip_group_check=True))
                    mmfn = lambda t, a=vst1, cc=c, c2=ci, pq=pav: t.matmul(
                        P2[pq][64:128, c2 * 512:c2 * 512 + 512], a,
                        att1[:, cc * 512:cc * 512 + 512], start=False, stop=True,
                        tile_position=(0, 64), skip_group_check=True)
                    if ci == 1:
                        t_av = inc("tensor", TS, mmfn)
                    else:
                        em("tensor", mmfn)
                wait("vector", TS, t_av)
                wait("scalar", TS, t_av)
                tv = inc("vector", VQ, lambda v, d=fmv(r_b, hp, 0, 1024): v.tensor_copy(d, P2[2][:]))
                ts2 = inc("scalar", SQ, lambda s2, d=fmv(r_b, hp, 1024, 1024): s2.copy(d, P2[3][:]))
                pair_free[2] = (VQ, tv)
                pair_free[3] = (SQ, ts2)
            wait("tensor", SQ, SQ.n)
            heads_ts = TS.n
            ao_done_v = VQ.n

            ckpt()
            wait("tensor", VQ, ao_done_v)
            wait("tensor", wout_rdy[0], wout_rdy[1])
            wait("tensor", bout_rdy[0], bout_rdy[1])
            wait("tensor", VQ, 1)
            wait("vector", TS, heads_ts)
            ln2_half_rdy = [None, None]

            def emit_ln2_half(half):
                wait("vector", TS, heads_ts)
                wait("scalar", TS, heads_ts)
                wait("sync", SPL, SPL.n)
                wait("sync", GAA, GAA.n)
                wait("sync", GAB, GAB.n)
                tiles = list(range(half * 8, half * 8 + 8))
                ln2_half_rdy[half] = emit_ln_phase(tiles, r_a)
                wait("sync", VQ, ln2_half_rdy[half][0])
                for t_i in tiles:
                    em("sync", lambda s, tt=t_i: s.dma_start(
                        out=spill_h[tt * 128:tt * 128 + 128, :],
                        in_=x_sb[:, tt * D:tt * D + D]).then_inc(SPL.h, 16))
                    SPL.n += 16

            for t_i in range(NT):
                pp = t_i % 2
                pair_wait(pp)
                for k in range(KD):
                    st_ap = fmv(r_b, k, t_i * 128, 128)
                    for ci in range(2):
                        em("tensor", lambda t, a=st_ap, kk=k, c2=ci, p_=pp: t.matmul(
                            P2[p_][:, c2 * 512:c2 * 512 + 512], a,
                            slot(wsl, 0)[:, kk * D + c2 * 512:kk * D + c2 * 512 + 512],
                            start=(kk == 0), stop=False))
                t_mma = inc("tensor", TS, lambda t, p_=pp: t.matmul(
                    P2[p_][:, 0:512], ones1[:], bout_r[:, 0:512], start=False, stop=True))
                t_mm = inc("tensor", TS, lambda t, p_=pp: t.matmul(
                    P2[p_][:, 512:1024], ones1[:], bout_r[:, 512:1024], start=False, stop=True))
                wait("vector", TS, t_mma)
                inc("vector", VQ, lambda v, p_=pp: v.bn_stats(st12[:, 0:6], P2[p_][:, 0:512]))
                wait("vector", TS, t_mm)
                t = inc("vector", VQ, lambda v, p_=pp: v.bn_stats(st12[:, 6:12], P2[p_][:, 512:1024]))
                wait("vector", VQ, t)
                t_ag = inc("vector", VQ, lambda v: v.bn_aggr(st2[:], st12[:]))
                wait("scalar", VQ, t_ag)
                t_ln = inc("scalar", SQ, lambda s: s.activation(lnv[:], st2[:, 1:2], AF.Ln, bias=epsv[:], scale=1.0))
                wait("scalar", SQ, t_ln)
                t_rr = inc("scalar", SQ, lambda s: s.activation(rr[:], lnv[:], AF.Exp, bias=0.0, scale=-0.5))
                wait("vector", SQ, t_rr)
                t_nm = inc("vector", VQ, lambda v: v.scalar_tensor_tensor(
                    nmr[:], st2[:, 0:1], -1.0, rr[:], ALU.mult, ALU.mult))
                xt_ = x_sb[:, t_i * D:t_i * D + D]
                if x_tile_rdy[t_i] is not None:
                    wait("scalar", x_tile_rdy[t_i][0], x_tile_rdy[t_i][1])
                wait("scalar", VQ, t_nm)
                t_b = inc("scalar", SQ, lambda s, d=xt_: s.activation(
                    d, d, AF.Identity, bias=nmr[:], scale=1.0))
                wait("vector", SQ, t_b)
                t_y = inc("vector", VQ, lambda v, d=xt_, p_=pp: v.scalar_tensor_tensor(
                    d, P2[p_][:], rr[:], d, ALU.mult, ALU.add))
                pair_free[pp] = (VQ, t_y)
                if t_i == 7:
                    emit_ln2_half(0)
            wout_ts = TS.n

            ckpt()
            emit_ln2_half(1)
            spill_done = SPL.n
            ln2_tr_ts = TS.n

            ckpt()
            moe_state = {"g_free_v": 0}

            wait("tensor", VQ, ln2_half_rdy[1][0])
            wait("tensor", SQ, ln2_half_rdy[1][1])
            wait("tensor", rt_rdy[0], rt_rdy[1])
            for t_i in range(NT):
                pair_wait(2)
                for k in range(KD - 1):
                    st_ap = fmv(r_a, k, t_i * 128, 128)
                    em("tensor", lambda t, a=st_ap, kk=k: t.matmul(
                        P2[2][:, 0:8], a, rt_sb[:, kk * E:kk * E + E],
                        start=(kk == 0), stop=False))
                st_ap = fmv(r_a, KD - 1, t_i * 128, 128)
                t_mm = inc("tensor", TS, lambda t, a=st_ap: t.matmul(
                    P2[2][:, 0:8], a, rt_sb[:, (KD - 1) * E:KD * E], start=False, stop=True))
                wait("vector", TS, t_mm)
                t_lg = inc("vector", VQ, lambda v, tt=t_i: v.tensor_copy(
                    lgA[:, tt * E:tt * E + E], P2[2][:, 0:8]))
                pair_free[2] = (VQ, t_lg)
            grp = lambda a: a.rearrange("p (g e) -> p g e", g=NT)
            bc = lambda a: a.unsqueeze(2).broadcast_to([128, NT, E])
            t = VQ.n
            wait("vector", VQ, t)
            t = inc("vector", VQ, lambda v: v.reduce_max(m1g[:], grp(lgA[:]), axis=AX.X))
            wait("vector", VQ, t)
            t = inc("vector", VQ, lambda v: v.tensor_tensor(
                grp(bufA[:]), grp(lgA[:]), bc(m1g[:]), ALU.subtract))
            wait("scalar", VQ, t)
            t_ex = inc("scalar", SQ, lambda s: s.activation(bufB, bufA[:], AF.Exp, bias=0.0, scale=1.0))
            wait("vector", SQ, t_ex)
            t = inc("vector", VQ, lambda v: v.tensor_tensor(
                grp(m1s[:]), grp(lgA[:]), bc(m1g[:]), ALU.is_equal))
            wait("vector", VQ, t)
            t = inc("vector", VQ, lambda v: v.scalar_tensor_tensor(
                wts[:], m1s[:], -1e30, lgA[:], ALU.mult, ALU.add))
            wait("vector", VQ, t)
            t = inc("vector", VQ, lambda v: v.reduce_max(m2g[:], grp(wts[:]), axis=AX.X))
            wait("vector", VQ, t)
            t = inc("vector", VQ, lambda v: v.tensor_tensor(
                grp(bufA[:]), grp(lgA[:]), bc(m2g[:]), ALU.is_ge))
            wait("vector", VQ, t)
            t_mskb = inc("vector", VQ, lambda v: v.tensor_copy(mskb[:], bufA[:]))
            wait("vector", VQ, t_mskb)
            t = inc("vector", VQ, lambda v: v.tensor_mul(lgA[:], bufB, bufA[:]))
            wait("vector", VQ, t)
            t = inc("vector", VQ, lambda v: v.reduce_sum(deng[:], grp(lgA[:]), axis=AX.X))
            wait("vector", VQ, t)
            t = inc("vector", VQ, lambda v: v.reciprocal(rdeng[:], deng[:]))
            wait("vector", VQ, t)
            t_wts = inc("vector", VQ, lambda v: v.tensor_tensor(
                grp(wts[:]), grp(lgA[:]), bc(rdeng[:]), ALU.mult))

            ckpt()
            ws1a = load_w(r_c, 0, ws1t_h[:, 0:1024], after=(TS, heads_ts))
            ws1b = load_w(r_c, 1, ws1t_h[:, 1024:2048], after=(TS, heads_ts))
            ws2a = load_w(wsl, 0, ws2t_h[0:1024, :], after=(TS, wout_ts))
            ws2b = load_w(wsl, 1, ws2t_h[1024:2048, :], after=(TS, qkv_ts))
            wait("tensor", ws1a[0], ws1a[1])
            wait("tensor", ws1b[0], ws1b[1])
            x_pref_base = LDX.n

            def emit_final(tc):
                    wait("vector", VQ, VQ.n)
                    tiles = [tc * 4 + i for i in range(4)]
                    for i, t_i in enumerate(tiles):
                        sl = x_sb[:, t_i * D:t_i * D + D]
                        em("vector", lambda v, s=sl, ii=i: v.bn_stats(stA[:, ii * 12:ii * 12 + 6], s[:, 0:512]))
                        em("vector", lambda v, s=sl, ii=i: v.bn_stats(stA[:, ii * 12 + 6:ii * 12 + 12], s[:, 512:1024]))
                    t = inc("vector", VQ, lambda v: v.engine_nop())
                    wait("vector", VQ, t)
                    for i in range(4):
                        em("vector", lambda v, ii=i: v.bn_aggr(st2A[:, ii * 2:ii * 2 + 2], stA[:, ii * 12:ii * 12 + 12]))
                    t_ag = inc("vector", VQ, lambda v: v.engine_nop())
                    wait("scalar", VQ, t_ag)
                    t_ln = inc("scalar", SQ, lambda s: s.activation(
                        lnvA[:, 0:4], st2A[:, 1:8:2], AF.Ln, bias=epsv[:], scale=1.0))
                    wait("scalar", SQ, t_ln)
                    t_rr = inc("scalar", SQ, lambda s: s.activation(
                        rrA[:, 0:4], lnvA[:, 0:4], AF.Exp, bias=0.0, scale=-0.5))
                    wait("vector", SQ, t_rr)
                    t_nm = inc("vector", VQ, lambda v: v.scalar_tensor_tensor(
                        nmrA[:, 0:4], st2A[:, 0:8:2], -1.0, rrA[:, 0:4], ALU.mult, ALU.mult))
                    wait("vector", VQ, t_nm)
                    wait("scalar", VQ, t_nm)
                    for i, t_i in enumerate(tiles):
                        par = t_i % 2
                        yt = yt0 if par == 0 else yt1
                        gsem = GAA if par == 0 else GAB
                        ssem = STA if par == 0 else STB
                        if yt_store_spq[par] is not None:
                            wait("scalar", yt_store_spq[par][0], yt_store_spq[par][1])
                        if par == 0 and xn_tmp_free[0] is not None:
                            wait("scalar", xn_tmp_free[0][0], xn_tmp_free[0][1])
                        if par == 0 and xn_tmp_free[1] is not None:
                            wait("scalar", xn_tmp_free[1][0], xn_tmp_free[1][1])
                        t_ap = inc("scalar", SQ, lambda s2, s=x_sb[:, t_i * D:t_i * D + D], d=yt, ii=i:
                                   s2.activation(d, s, AF.Identity, bias=nmrA[:, ii:ii + 1], scale=rrA[:, ii:ii + 1]))
                        wait("gpsimd", gsem, gsem.n)
                        wait("gpsimd", SPL, spill_done)
                        wait("gpsimd", SQ, t_ap)
                        em("gpsimd", lambda g, tt_=t_i, d=yt, gs=gsem: g.dma_start(
                            out=d[:], in_=spill_h[tt_ * 128:tt_ * 128 + 128, :],
                            accum_op=ALU.add).then_inc(gs.h, 16))
                        gsem.n += 16
                        wait("sync", ssem, ssem.n)
                        wait("sync", gsem, gsem.n)
                        em("sync", lambda s, tt_=t_i, bb_=b, d=yt, ss=ssem: s.dma_start(
                            out=out_h[bb_ * T + tt_ * 128:bb_ * T + tt_ * 128 + 128, :],
                            in_=d[:]).then_inc(ss.h, 16))
                        ssem.n += 16
                        yt_store_spq[par] = (ssem, ssem.n)
                        if b + 1 < B_LOC:
                            wait("sync", LDX, LDX.n)
                            wait("sync", SQ, t_ap)
                            em("sync", lambda s, tt_=t_i, bb_=b: s.dma_start(
                                out=x_sb[:, tt_ * D:tt_ * D + D],
                                in_=x_h[(bb_ + 1) * T + tt_ * 128:(bb_ + 1) * T + tt_ * 128 + 128, :]
                                ).then_inc(LDX.h, 16))
                            LDX.n += 16
                            x_tile_rdy[t_i] = (LDX, LDX.n)



            tmv = lambda i: r_b[:, i * 1024:(i + 1) * 1024]
            xnsv = lambda k: r_b[:, 2048 + k * SLOT:2048 + (k + 1) * SLOT]
            ystv = lambda i: r_b[:, 4096 + i * 1024:4096 + (i + 1) * 1024]
            gbv = lambda j: r_b[:, j * 2048 + 1024:(j + 1) * 2048]
            h1sv = lambda ft: att1[:, ft * SLOT:(ft + 1) * SLOT]
            pc8 = lambda kk: kk * 128 if kk < 4 else 1024 + (kk - 4) * 128
            cstate = {}
            xsc_free = [0, 0]

            def emit_cnt_mms():
                pair_wait(3)
                wait("tensor", VQ, t_mskb)
                wait("tensor", cst_rdy[0], cst_rdy[1])
                wait("tensor", VQ, t_onec)
                em("tensor", lambda t: t.matmul(P2[3][:, 0:128], ltb[:], mskb[:], start=True, stop=True))
                cstate["cnt_ts"] = inc("tensor", TS, lambda t: t.matmul(
                    P2[3][0:1, 128:256], onecb[:], mskb[:], start=True, stop=True))

            def emit_counts_chain():
                wait("vector", TS, cstate["cnt_ts"])
                t = inc("vector", VQ, lambda v: v.tensor_copy(posf[:], P2[3][:, 0:128]))
                wait("vector", VQ, t)
                t = inc("vector", VQ, lambda v: v.tensor_copy(ctr[:], P2[3][0:1, 128:256]))
                wait("vector", VQ, t)
                t = inc("vector", VQ, lambda v: v.tensor_copy(cta[:], ctr[:]))
                wait("vector", VQ, t)
                for s_ in (64, 32, 16, 8):
                    t = inc("vector", VQ, lambda v, ss=s_: v.tensor_tensor(
                        cta[:, 0:ss], cta[:, 0:ss], cta[:, ss:2 * ss], ALU.add))
                    wait("vector", VQ, t)
                r3 = rowb[:, 0:128].rearrange("p (a c) -> p a c", a=8)
                t = inc("vector", VQ, lambda v: v.tensor_tensor(
                    r3, cta[:, 0:8].unsqueeze(2).broadcast_to([1, 8, 16]),
                    cstr[:, 0:16].unsqueeze(1).broadcast_to([1, 8, 16]), ALU.is_gt))
                wait("vector", VQ, t)
                t = inc("vector", VQ, lambda v: v.reduce_sum(reg8[:], r3, axis=AX.X))
                wait("vector", VQ, t)
                t = inc("vector", VQ, lambda v: v.tensor_scalar(
                    reg8[:], reg8[:], float(SLOT), None, ALU.mult))
                wait("vector", VQ, t)
                em("vector", lambda v: v.memset(pA8[:], 0.0))
                t = inc("vector", VQ, lambda v: v.tensor_copy(pA8[:, 1:8], reg8[:, 0:7]))
                wait("vector", VQ, t)
                pp_ = [pA8, pB8]
                for i, s_ in enumerate((1, 2, 4)):
                    src, dst = pp_[i % 2], pp_[(i + 1) % 2]
                    em("vector", lambda v, s2=src, d2=dst, ss=s_: v.tensor_copy(d2[:, 0:ss], s2[:, 0:ss]))
                    t = inc("vector", VQ, lambda v, s2=src, d2=dst, ss=s_: v.tensor_tensor(
                        d2[:, ss:8], s2[:, ss:8], s2[:, 0:8 - ss], ALU.add))
                    wait("vector", VQ, t)
                em("vector", lambda v: v.memset(tbA[:], 0.0))
                t = inc("vector", VQ, lambda v: v.tensor_copy(tbA[:, 8:128], ctr[:, 0:120]))
                wait("vector", VQ, t)
                tt_ = [tbA, tbB]
                for i, s_ in enumerate((8, 16, 32, 64)):
                    src, dst = tt_[i % 2], tt_[(i + 1) % 2]
                    em("vector", lambda v, s2=src, d2=dst, ss=s_: v.tensor_copy(d2[:, 0:ss], s2[:, 0:ss]))
                    t = inc("vector", VQ, lambda v, s2=src, d2=dst, ss=s_: v.tensor_tensor(
                        d2[:, ss:128], s2[:, ss:128], s2[:, 0:128 - ss], ALU.add))
                    wait("vector", VQ, t)
                t = inc("vector", VQ, lambda v: v.tensor_tensor(
                    tbA[:].rearrange("p (t e) -> p t e", t=NT),
                    tbA[:].rearrange("p (t e) -> p t e", t=NT),
                    pB8[:].unsqueeze(1).broadcast_to([1, NT, 8]), ALU.add))
                wait("vector", VQ, t)
                s3 = rowb[:, 0:NSLOT * 8].rearrange("p (a c) -> p a c", a=NSLOT)
                t = inc("vector", VQ, lambda v: v.tensor_tensor(
                    s3, cstr[:, 32:32 + NSLOT].unsqueeze(2).broadcast_to([1, NSLOT, 8]),
                    pB8[:].unsqueeze(1).broadcast_to([1, NSLOT, 8]), ALU.is_gt))
                wait("vector", VQ, t)
                t = inc("vector", VQ, lambda v: v.reduce_sum(sexpr[:, 0:NSLOT], s3, axis=AX.X))
                wait("vector", VQ, t)
                t = inc("vector", VQ, lambda v: v.tensor_scalar(
                    sexpr[:, 0:NSLOT], sexpr[:, 0:NSLOT], -1.0, None, ALU.add))
                cstate["tb_v"] = t

            def emit_broadcasts():
                t_of = inc("vector", VQ, lambda v: v.memset(ones1f, 1.0))
                wait("tensor", VQ, cstate["tb_v"])
                wait("tensor", VQ, t_of)
                em("tensor", lambda t: t.matmul(P2[3][:, 256:384], ones1f, tbA, start=True, stop=True))
                cstate["bb_ts"] = inc("tensor", TS, lambda t: t.matmul(
                    P2[3][:, 384:384 + NSLOT], ones1f, sexpr[:, 0:NSLOT], start=True, stop=True))

            def emit_dest_chain():
                g3 = lambda a: a.rearrange("p (t e) -> p t e", t=NT)
                wait("vector", TS, cstate["bb_ts"])
                t = inc("vector", VQ, lambda v: v.tensor_tensor(
                    dif[:], posf[:], P2[3][:, 256:384], ALU.add))
                wait("vector", VQ, t)
                t = inc("vector", VQ, lambda v: v.tensor_scalar(
                    widxf[:, 0:NSLOT], P2[3][:, 384:384 + NSLOT], 128.0, iop[:, 0:1], ALU.mult, ALU.add))
                wait("vector", VQ, t)
                pair_free[3] = (VQ, t)
                t = inc("vector", VQ, lambda v: v.tensor_copy(widxi[:, 0:NSLOT], widxf[:, 0:NSLOT]))
                wait("vector", VQ, t)
                t = inc("vector", VQ, lambda v: v.tensor_mul(bufB, m1s[:], dif[:]))
                wait("vector", VQ, t)
                t = inc("vector", VQ, lambda v: v.reduce_sum(dst1f[:], g3(bufB), axis=AX.X))
                wait("vector", VQ, t)
                t = inc("vector", VQ, lambda v: v.tensor_mul(bufB, bufA[:], dif[:]))
                wait("vector", VQ, t)
                t = inc("vector", VQ, lambda v: v.reduce_sum(dst2f[:], g3(bufB), axis=AX.X))
                wait("vector", VQ, t)
                t = inc("vector", VQ, lambda v: v.tensor_tensor(
                    dst2f[:], dst2f[:], dst1f[:], ALU.subtract))
                wait("vector", VQ, t)
                wait("vector", VQ, t_wts)
                t = inc("vector", VQ, lambda v: v.tensor_mul(bufB, m1s[:], wts[:]))
                wait("vector", VQ, t)
                t = inc("vector", VQ, lambda v: v.reduce_sum(w1tok[:], g3(bufB), axis=AX.X))
                wait("vector", VQ, t)
                t = inc("vector", VQ, lambda v: v.tensor_scalar(
                    w2tok[:], w1tok[:], -1.0, 1.0, ALU.mult, ALU.add))
                wait("vector", VQ, t)
                t = inc("vector", VQ, lambda v: v.tensor_copy(d1i[:], dst1f[:]))
                wait("vector", VQ, t)
                t = inc("vector", VQ, lambda v: v.tensor_copy(d2i[:], dst2f[:]))
                wait("vector", VQ, t)
                t = inc("vector", VQ, lambda v: v.tensor_copy(iguard[:], d2i[:, 0:1]))
                wait("vector", VQ, t)
                t = inc("vector", VQ, lambda v: v.tensor_copy(iguard[:], widxi[:, 0:1]))
                cstate["guard_v"] = t

            def emit_xtrans(t_i):
                par = t_i % 2
                xst = att0[:, par * 1024:(par + 1) * 1024]
                pp = par
                pair_wait(pp)
                wait("tensor", id_rdy[0], id_rdy[1])
                pview = P2[pp][:].bitcast(BF16)
                for k in range(KD):
                    fn = lambda t, s2=fmv(r_a, k, t_i * 128, 128), pv=pview, cc=pc8(k): \
                        t.transpose(pv[:, cc:cc + 128], s2, idb[:])
                    if k == KD - 1:
                        t_tr = inc("tensor", TS, fn)
                    else:
                        em("tensor", fn)
                wait("vector", TS, t_tr)
                wait("scalar", TS, t_tr)
                wait("vector", XSC, xsc_free[par])
                wait("scalar", XSC, xsc_free[par])
                tv = ts2 = 0
                for k in range(KD):
                    dst = xst[:, k * 128:(k + 1) * 128]
                    if k < 4:
                        fn_v = lambda v, d=dst, pv=pview, cc=pc8(k): v.tensor_copy(d, pv[:, cc:cc + 128])
                        if k == 3:
                            tv = inc("vector", VQ, fn_v)
                        else:
                            em("vector", fn_v)
                    else:
                        fn_s = lambda sc, d=dst, pv=pview, cc=pc8(k): sc.copy(d, pv[:, cc:cc + 128])
                        if k == KD - 1:
                            ts2 = inc("scalar", SQ, fn_s)
                        else:
                            em("scalar", fn_s)
                pair_free[pp] = [(VQ, tv), (SQ, ts2)]
                wait("gpsimd", VQ, tv)
                wait("gpsimd", SQ, ts2)
                wait("gpsimd", VQ, cstate["guard_v"])
                for di_ in (d1i, d2i):
                    em("gpsimd", lambda g, c=di_[:, t_i:t_i + 1], s2=xst:
                       g.indirect_dma_start(out=xnc_h, out_offset=IndirectOffsetOnAxis(ap=c, axis=0),
                                            in_=s2, in_offset=None).then_inc(XSC.h, 16))
                    XSC.n += 16
                xsc_free[par] = XSC.n

            for tc in range(4):
                tok0 = tc * 512
                wait("vector", TS, wout_ts)
                wait("scalar", TS, wout_ts)
                for fst in range(NFS):
                    p_ = fst % 2
                    pair_wait(p_)
                    wsl1 = slot(r_c, fst // 8)
                    fo = (fst % 8) * 128
                    for k in range(KD - 1):
                        st_ap = wsl1[:, k * D + fo:k * D + fo + 128]
                        em("tensor", lambda t, a=st_ap, kk=k, pq=p_, t0=tok0: t.matmul(
                            P2[pq][:, 0:512], a, fmv(r_a, kk, t0, 512), start=(kk == 0), stop=False))
                    st_ap = wsl1[:, (KD - 1) * D + fo:(KD - 1) * D + fo + 128]
                    t_mm = inc("tensor", TS, lambda t, a=st_ap, pq=p_, t0=tok0: t.matmul(
                        P2[pq][:, 0:512], a, fmv(r_a, KD - 1, t0, 512), start=False, stop=True))
                    if tc == 3 and fst == 7:
                        sh_h0_ts = TS.n
                    if tc == 3 and fst == 15:
                        sh_h1_ts = TS.n
                    wait("scalar", TS, t_mm)
                    wait("scalar", VQ, moe_state["g_free_v"])
                    t_g = inc("scalar", SQ, lambda s, pq=p_: s.activation(
                        g_buf[:, 0:512], P2[pq][:, 0:512], AF.Relu, bias=0.0, scale=1.0))
                    pair_free[p_] = (SQ, t_g)
                    wait("vector", SQ, t_g)
                    moe_state["g_free_v"] = inc("vector", VQ, lambda v, f2=fst: v.tensor_tensor(
                        sh_ap(f2, 0, 512), g_buf[:, 0:512], g_buf[:, 0:512], ALU.mult))
                if tc == 0:
                    emit_cnt_mms()
                    emit_counts_chain()
                elif tc == 1:
                    emit_broadcasts()
                    emit_dest_chain()
                elif tc == 2:
                    for i in range(NT):
                        emit_xtrans(i)
                sh_v = VQ.n
                wait("tensor", VQ, sh_v)
                wait("tensor", ws2a[0], ws2a[1])
                wait("tensor", ws2b[0], ws2b[1])
                for tt in range(4):
                    ttile = tc * 4 + tt
                    p_ = 2 if tc < 2 else 2 + tt % 2
                    pair_wait(p_)
                    for fst in range(NFS):
                        st_ap = sh_ap(fst, tt * 128, 128)
                        w2s = slot(wsl, fst // 8)
                        fo = fst % 8
                        for ci in range(2):
                            if fst == NFS - 1 and ci == 1:
                                break
                            em("tensor", lambda t, a=st_ap, w=w2s, f2=fo, c2=ci, pq=p_,
                               fi=(fst == 0), la=(fst == NFS - 1):
                               t.matmul(P2[pq][:, c2 * 512:c2 * 512 + 512], a,
                                        w[:, f2 * D + c2 * 512:f2 * D + c2 * 512 + 512],
                                        start=fi, stop=la))
                    st_ap = sh_ap(NFS - 1, tt * 128, 128)
                    w2s = slot(wsl, 1)
                    t_mm = inc("tensor", TS, lambda t, a=st_ap, w=w2s, pq=p_: t.matmul(
                        P2[pq][:, 512:1024], a, w[:, 7 * D + 512:7 * D + 1024], start=False, stop=True))
                    wait("vector", TS, t_mm)
                    wait("vector", SPL, spill_done)
                    dst = x_sb[:, ttile * D:ttile * D + D]
                    tv = inc("vector", VQ, lambda v, d=dst, pq=p_: v.tensor_copy(d, P2[pq][:]))
                    pair_free[p_] = (VQ, tv)
            xsc_barrier = XSC.n

            ckpt()
            sh_all_ts = TS.n
            slot_h1_ts = {}
            slot_y_ts = {}
            trans_ts = {-1: 0}
            tmr_val = {}
            ywr_prev = [YWR.n, YWR.n]
            wrdy = {}

            def gather_w(s):
                h = s % 2
                sem1 = LCA if h == 0 else LCB
                sem2 = LWA if h == 0 else LWB
                aft1 = (TS, slot_h1_ts[s - 2]) if s >= 2 else (TS, (sh_h0_ts if h == 0 else sh_h1_ts))
                aft2 = (TS, slot_y_ts[s - 2]) if s >= 2 else (TS, sh_all_ts)
                out = []
                for (sem, dst_t, src, aft) in ((sem1, r_c, w1s_h, aft1), (sem2, wsl, w2s_h, aft2)):
                    wait("gpsimd", sem, sem.n)
                    wait("gpsimd", WCS, wcs_done)
                    wait("gpsimd", VQ, cstate["guard_v"])
                    wait("gpsimd", aft[0], aft[1])
                    em("gpsimd", lambda g, d=slot(dst_t, h), s2=src, c=widxi[:, s:s + 1], ss=sem:
                       g.indirect_dma_start(out=d, out_offset=None, in_=s2,
                                            in_offset=IndirectOffsetOnAxis(ap=c, axis=0)).then_inc(ss.h, 16))
                    sem.n += 16
                    out.append((sem, sem.n))
                return out

            def read_tm(s):
                wait("sync", XSC, xsc_barrier)
                wait("sync", TMR, TMR.n)
                wait("sync", TS, trans_ts[s - 1])
                for i in range(2):
                    em("sync", lambda sy, d=tmv(i), r0=s * SLOT + i * 128:
                       sy.dma_start(out=d, in_=xnc_h[r0:r0 + 128, :]).then_inc(TMR.h, 16))
                    TMR.n += 16
                tmr_val[s] = TMR.n

            wrdy[0] = gather_w(0)
            read_tm(0)
            wrdy[1] = gather_w(1)
            for s in range(NSLOT):
                h = s % 2
                wait("tensor", TMR, tmr_val[s])
                pair_wait(0)
                pair_wait(1)
                pviews = [P2[0][:].bitcast(BF16), P2[1][:].bitcast(BF16)]
                for i in range(2):
                    for k in range(KD):
                        j = i * KD + k
                        fn = lambda t, s2=r_b[:, i * 1024 + k * 128:i * 1024 + k * 128 + 128], \
                                pv=pviews[j // 8], cc=pc8(j % 8): t.transpose(pv[:, cc:cc + 128], s2, idb[:])
                        if j == 15:
                            t_tr = inc("tensor", TS, fn)
                        elif j == 7:
                            t_trA = inc("tensor", TS, fn)
                        else:
                            em("tensor", fn)
                trans_ts[s] = t_tr
                wait("vector", TS, t_trA)
                wait("scalar", TS, t_tr)
                if s >= 1:
                    wait("vector", TS, slot_h1_ts[s - 1])
                    wait("scalar", TS, slot_h1_ts[s - 1])
                tv_x = ts_x = 0
                for i in range(2):
                    for k in range(KD):
                        j = i * KD + k
                        dst = xnsv(k)[:, i * 128:(i + 1) * 128]
                        if j < 8:
                            fn_v = lambda v, d=dst, pv=pviews[0], cc=pc8(j % 8): v.tensor_copy(d, pv[:, cc:cc + 128])
                            if j == 7:
                                tv_x = inc("vector", VQ, fn_v)
                            else:
                                em("vector", fn_v)
                        else:
                            fn_s = lambda sc, d=dst, pv=pviews[1], cc=pc8(j % 8): sc.copy(d, pv[:, cc:cc + 128])
                            if j == 15:
                                ts_x = inc("scalar", SQ, fn_s)
                            else:
                                em("scalar", fn_s)
                pair_free[0] = (VQ, tv_x)
                pair_free[1] = (SQ, ts_x)
                if s + 1 < NSLOT:
                    read_tm(s + 1)
                wait("tensor", wrdy[s][0][0], wrdy[s][0][1])
                wait("tensor", VQ, tv_x)
                wait("tensor", SQ, ts_x)
                sq_v = [0] * KD
                for ft in range(KD):
                    pq = 2 + ft % 2
                    pair_wait(pq)
                    for k in range(KD):
                        st_ap = slot(r_c, h)[:, k * D + ft * 128:k * D + ft * 128 + 128]
                        fn = lambda t, a=st_ap, kk=k, pq2=pq: t.matmul(
                            P2[pq2][:, 0:SLOT], a, xnsv(kk), start=(kk == 0), stop=(kk == KD - 1))
                        if k == KD - 1:
                            t_mm = inc("tensor", TS, fn)
                        else:
                            em("tensor", fn)
                    wait("scalar", TS, t_mm)
                    wait("scalar", VQ, moe_state["g_free_v"])
                    t_g = inc("scalar", SQ, lambda s2, pq2=pq: s2.activation(
                        g_buf[:, 0:SLOT], P2[pq2][:, 0:SLOT], AF.Relu, bias=0.0, scale=1.0))
                    pair_free[pq] = (SQ, t_g)
                    wait("vector", SQ, t_g)
                    if s >= 1 and ft == 0:
                        wait("vector", TS, slot_y_ts[s - 1])
                    moe_state["g_free_v"] = inc("vector", VQ, lambda v, f2=ft: v.tensor_tensor(
                        h1sv(f2), g_buf[:, 0:SLOT], g_buf[:, 0:SLOT], ALU.mult))
                    sq_v[ft] = moe_state["g_free_v"]
                slot_h1_ts[s] = TS.n
                h1_v = VQ.n
                wait("tensor", wrdy[s][1][0], wrdy[s][1][1])
                for tt in range(2):
                    pq = 2 + tt
                    pair_wait(pq)
                    for ft in range(KD):
                        wait("tensor", VQ, sq_v[ft])
                        st_ap = h1sv(ft)[:, tt * 128:tt * 128 + 128]
                        for ci in range(2):
                            fn = lambda t, a=st_ap, kk=ft, c2=ci, pq2=pq, hh=h: t.matmul(
                                P2[pq2][:, c2 * 512:c2 * 512 + 512], a,
                                slot(wsl, hh)[:, kk * D + c2 * 512:kk * D + c2 * 512 + 512],
                                start=(kk == 0), stop=(kk == KD - 1))
                            if ft == KD - 1 and ci == 1:
                                t_mm = inc("tensor", TS, fn)
                            else:
                                em("tensor", fn)
                    wait("vector", TS, t_mm)
                    wait("vector", YWR, ywr_prev[tt])
                    tv = inc("vector", VQ, lambda v, d=ystv(tt), pq2=pq: v.tensor_copy(d, P2[pq2][:]))
                    pair_free[pq] = (VQ, tv)
                    wait("sync", VQ, tv)
                    em("sync", lambda sy, d=ystv(tt), r0=s * SLOT + tt * 128:
                       sy.dma_start(out=yc_h[r0:r0 + 128, :], in_=d).then_inc(YWR.h, 16))
                    YWR.n += 16
                    ywr_prev[tt] = YWR.n
                slot_y_ts[s] = TS.n
                if s + 2 < NSLOT:
                    wrdy[s + 2] = gather_w(s + 2)
            ywr_done = YWR.n

            ckpt()
            if b + 1 < B_LOC:
                wq_rdy = load_w(wsl, 0, wq_h, after=(TS, slot_y_ts[NSLOT - 1]))
                r_cf = r_c[:].bitcast(F32)
                rcx = lambda t: r_cf[:, (t % 8) * D:((t % 8) + 1) * D]
                rcx_rdy = {}
                wait("sync", TS, slot_h1_ts[NSLOT - 1])
                wait("sync", LCA, LCA.n)
                wait("sync", LCB, LCB.n)
                for t_i in range(8):
                    em("sync", lambda s, d=rcx(t_i), r0=(b + 1) * T + t_i * 128:
                       s.dma_start(out=d, in_=x_h[r0:r0 + 128, :]).then_inc(LDX.h, 16))
                    LDX.n += 16
                    rcx_rdy[t_i] = (LDX, LDX.n)
                ln1_h0 = emit_ln_phase(list(range(8)), r_a, aeng="scalar",
                                       src=rcx, src_rdy=lambda t: rcx_rdy[t])
                wkv_rdy = load_w(wsl, 1, wkv_h, after=(TS, slot_y_ts[NSLOT - 1]))

            gfree = [0] * 6

            def emit_recombine(tiles):
                for t_i in tiles:
                    b1 = (2 * t_i) % 6
                    b2 = (2 * t_i + 1) % 6
                    wait("gpsimd", YWR, ywr_done)
                    wait("gpsimd", VQ, cstate["guard_v"])
                    for (bj, di_) in ((b1, d1i), (b2, d2i)):
                        wait("gpsimd", VQ, gfree[bj])
                        em("gpsimd", lambda g, d=gbv(bj), c=di_[:, t_i:t_i + 1]:
                           g.indirect_dma_start(out=d, out_offset=None, in_=yc_h,
                                                in_offset=IndirectOffsetOnAxis(ap=c, axis=0)).then_inc(YGQ.h, 16))
                        YGQ.n += 16
                    ygv = YGQ.n
                    dst = x_sb[:, t_i * D:(t_i + 1) * D]
                    wait("vector", YGQ, ygv)
                    t = inc("vector", VQ, lambda v, d=dst, g1=gbv(b1), w=w1tok[:, t_i:t_i + 1]:
                            v.scalar_tensor_tensor(d, g1, w, d, ALU.mult, ALU.add))
                    wait("vector", VQ, t)
                    t2 = inc("vector", VQ, lambda v, d=dst, g2=gbv(b2), w=w2tok[:, t_i:t_i + 1]:
                             v.scalar_tensor_tensor(d, g2, w, d, ALU.mult, ALU.add))
                    wait("vector", VQ, t2)
                    gfree[b1] = t2
                    gfree[b2] = t2

            emit_recombine(range(0, 4))
            emit_final(0)
            emit_recombine(range(4, 8))
            emit_final(1)
            if b + 1 < B_LOC:
                g_ts = slot_y_ts[NSLOT - 1]
                gsems = ((LCA, LCA.n), (LCB, LCB.n), (YWR, YWR.n))
                emit_proj_half(0, r_b, SCALE, 0, ln1_h0[0], ln1_h0[1], wq_rdy,
                               guard_ts=g_ts, guard_sems=gsems, evac_eng="scalar")
                wait("sync", VQ, ln1_h0[0])
                wait("sync", SQ, ln1_h0[1])
                for t_i in range(8, NT):
                    em("sync", lambda s, d=rcx(t_i), r0=(b + 1) * T + t_i * 128:
                       s.dma_start(out=d, in_=x_h[r0:r0 + 128, :]).then_inc(LDX.h, 16))
                    LDX.n += 16
                    rcx_rdy[t_i] = (LDX, LDX.n)
                ln1_h1 = emit_ln_phase(list(range(8, NT)), r_a, aeng="scalar",
                                       src=rcx, src_rdy=lambda t: rcx_rdy[t])
            emit_recombine(range(8, 12))
            emit_final(2)
            if b + 1 < B_LOC:
                emit_proj_half(1, r_c, 1.0, 0, ln1_h0[0], ln1_h0[1], wkv_rdy,
                               guard_sems=((VQ, ln1_h1[0]), (SQ, ln1_h1[1])),
                               evac_eng="scalar")
            emit_recombine(range(12, NT))
            emit_final(3)

            ckpt()
            if b + 1 < B_LOC:
                emit_proj_half(0, r_b, SCALE, 1, ln1_h1[0], ln1_h1[1], wq_rdy)
                emit_proj_half(1, r_c, 1.0, 1, ln1_h1[0], ln1_h1[1], wkv_rdy)
                carry = {"qkv_ts": TS.n, "kv_evac_v": VQ.n, "kv_evac_s": SQ.n}
                x_ready[b + 1] = (LDX, LDX.n)
        except _StopEmission:
            if os.environ.get("MK_DUMP"):
                wait("sync", VQ, VQ.n)
                wait("sync", SQ, SQ.n)
                wait("sync", TS, TS.n)
                wait("sync", LDX, LDX.n)
                wait("sync", GAA, GAA.n)
                wait("sync", GAB, GAB.n)
                em("sync", lambda s: s.dma_start(
                    out=out_h[0:T, :].rearrange("(n p) d -> p n d", p=128),
                    in_=x_sb[:].rearrange("p (n d) -> p n d", n=NT)).then_inc(STA.h, 16))
                STA.n += 16

        for s_ in (STA, STB, LDX, SPL, XSC, WCS, TRV, TRW, YWR, YGQ, TMR, GAA, GAB, LWA, LWB, LCA, LCB, LRT, TS, VQ, SQ):
            em("gpsimd", lambda g, ss=s_: g.wait_ge(ss.h, ss.n))

        with nc.Block("mk", no_gpsimd_drain=True) as blk:
            blk.tensor(lambda e: [f(e) for f in P.prog["tensor"]])
            blk.vector(lambda e: [f(e) for f in P.prog["vector"]])
            blk.scalar(lambda e: [f(e) for f in P.prog["scalar"]])
            blk.sync(lambda e: [f(e) for f in P.prog["sync"]])
            blk.gpsimd(lambda e: [f(e) for f in P.prog["gpsimd"]])

    return nc


_NC_CACHE = None


def kernel(**inputs):
    global _NC_CACHE
    x = np.asarray(inputs["L0_attn_hidden_states"], dtype=np.float32)
    wq = np.ascontiguousarray(np.asarray(inputs["L0_attn_Wq"], dtype=np.float32))
    wkv = np.ascontiguousarray(np.asarray(inputs["L0_attn_Wkv"], dtype=np.float32))
    wout = np.ascontiguousarray(np.asarray(inputs["L0_attn_Wout"], dtype=np.float32))
    bout = np.ascontiguousarray(np.asarray(inputs["L0_attn_bout"], dtype=np.float32).reshape(1, D))
    rt = np.ascontiguousarray(np.asarray(inputs["L0_ffn_w_router"], dtype=np.float32).T)
    w1t = np.ascontiguousarray(np.asarray(inputs["L0_ffn_w1_stack"], dtype=np.float32).transpose(0, 2, 1))
    w2t = np.ascontiguousarray(np.asarray(inputs["L0_ffn_w2_stack"], dtype=np.float32).transpose(0, 2, 1))
    ws1t = np.ascontiguousarray(np.asarray(inputs["L0_ffn_ws1"], dtype=np.float32).T)
    ws2t = np.ascontiguousarray(np.asarray(inputs["L0_ffn_ws2"], dtype=np.float32).T)
    ident = np.eye(128, dtype=np.float32)
    ltri = np.ascontiguousarray(np.triu(np.ones((128, 128), dtype=np.float32), 1))
    cstr = np.zeros((1, 64), dtype=np.float32)
    cstr[0, 0:16] = np.arange(16, dtype=np.float32) * SLOT + 0.5
    cstr[0, 32:32 + NSLOT] = np.arange(NSLOT, dtype=np.float32) * SLOT + 0.5
    iotap = np.zeros((128, 2), dtype=np.float32)
    iotap[:, 0] = np.arange(128, dtype=np.float32)
    iotap[:, 1] = 1.0

    if _NC_CACHE is None:
        _NC_CACHE = build_program()
    nc = _NC_CACHE
    in_maps = []
    for c in range(NCORE):
        shard = np.ascontiguousarray(x[c * B_LOC:(c + 1) * B_LOC].reshape(B_LOC * T, D))
        in_maps.append({"x": shard, "wq": wq, "wkv": wkv, "wout": wout, "boutr": bout,
                        "rt": rt, "w1t": w1t, "w2t": w2t, "ws1t": ws1t, "ws2t": ws2t,
                        "ident": ident, "ltri": ltri, "cstr": cstr, "iotap": iotap})
    res = run_bass_kernel_spmd(nc, in_maps, core_ids=list(range(NCORE)))
    outs = [np.asarray(r["out"]).reshape(B_LOC, T, D) for r in res.results]
    return np.concatenate(outs, axis=0)
